# version 38
# baseline (speedup 1.0000x reference)
import math
import numpy as np
from contextlib import ExitStack

import concourse.bass as bass
import concourse.mybir as mybir
from concourse.bass_utils import run_bass_kernel_spmd

F32 = mybir.dt.float32
BF16 = mybir.dt.bfloat16
AF = mybir.ActivationFunctionType
ALU = mybir.AluOpType
AX = mybir.AxisListType

R_SOFTPLUS_0 = 1.442695041
NEG_C = -60.0
LN_EPS = 1e-5
ROUTED_SCALE = 2.5


def full_cfg():
    return dict(S=4096, D=1024, H=16, KVH=4, Dh=64, E=8, I=512, SHI=1024,
                QC=512, TCW=512, TC=1024, TS=256)


class Sem:
    def __init__(self, nc, name, dma=False):
        self.h = nc.alloc_semaphore(name)
        self.n = 0
        self.step = 16 if dma else 1

    def inc(self, instr, k=1):
        instr.then_inc(self.h, k * self.step)
        self.n += k * self.step
        return self.n

    def wait(self, eng, val=None):
        v = self.n if val is None else val
        if v > 0:
            eng.wait_ge(self.h, v)


class B:
    def __init__(self, nc, cfg):
        self.nc = nc
        c = dict(cfg)
        self.c = c
        c["kt_n"] = c["S"] // 128
        c["rep"] = c["H"] // c["KVH"]
        c["G"] = c["KVH"]
        c["di_n"] = c["D"] // 128
        c["nQ"] = c["H"] // 2
        c["nK"] = c["KVH"] // 2
        c["do_n"] = c["nQ"] + 2 * c["nK"]
        c["DO"] = c["do_n"] * 128
        c["n_qc"] = c["S"] // c["QC"]
        c["n_tcw"] = c["S"] // c["TCW"]
        c["ic_n"] = c["I"] // 128
        c["hc_n"] = c["D"] // 128
        c["n_tc"] = c["S"] // c["TC"]
        c["n_ts"] = c["TC"] // c["TS"]
        c["tsub_n"] = c["TC"] // 128
        c["n_sh"] = c["SHI"] // c["I"]
        c["nE"] = c["E"] + c["n_sh"]
        assert c["E"] == 8 and c["Dh"] == 64 and c["nQ"] == c["di_n"]
        self.TN = Sem(nc, "TN")
        self.VC = Sem(nc, "VC")
        self.SC = Sem(nc, "SC")
        self.GP = Sem(nc, "GP")
        self._epoch = {}

    def V(self, instr_fn):
        self.VC.wait(self.nc.vector)
        return self.VC.inc(instr_fn())

    def S(self, instr_fn):
        self.SC.wait(self.nc.scalar)
        return self.SC.inc(instr_fn())

    def G(self, instr_fn):
        self.GP.wait(self.nc.gpsimd)
        return self.GP.inc(instr_fn())

    def soft_barrier(self):
        nc = self.nc
        for eng in (nc.tensor, nc.vector, nc.scalar, nc.gpsimd, nc.sync):
            for s in (self.TN, self.VC, self.SC, self.GP):
                s.wait(eng)

    def epoch(self, name):
        self._epoch[name] = set()

    def mm_start(self, name, col_start, col_end):
        banks = set(range(col_start // 512, (col_end + 511) // 512))
        st = self._epoch[name]
        new = banks - st
        st |= banks
        return bool(new)


def _ap2(t, off, dims):
    a = t.ap()
    return bass.AP(a.tensor, off, [[a.ap[0][0], a.ap[0][1]]] + dims)


def _emit_ln(b, src, dst, sqs, lnt, consts, D):
    nc = b.nc
    vc, sc = nc.vector, nc.scalar
    CEPS, CNH = 0, 1
    b.V(lambda: vc.reduce_sum(lnt[:, 0:1], src[:, 0:D], axis=AX.X))
    b.VC.wait(sc)
    b.S(lambda: sc.activation(sqs[:, 0:D], src[:, 0:D], AF.Square,
                              accum_out=lnt[:, 1:2]))
    b.SC.wait(vc)
    b.V(lambda: vc.tensor_scalar_mul(lnt[:, 0:1], lnt[:, 0:1], 1.0 / D))
    b.V(lambda: vc.tensor_scalar_mul(lnt[:, 1:2], lnt[:, 1:2], 1.0 / D))
    b.V(lambda: vc.tensor_tensor(lnt[:, 4:5], lnt[:, 0:1], lnt[:, 0:1], ALU.mult))
    b.V(lambda: vc.tensor_sub(lnt[:, 2:3], lnt[:, 1:2], lnt[:, 4:5]))
    b.VC.wait(sc)
    b.S(lambda: sc.activation(lnt[:, 3:4], lnt[:, 2:3], AF.Ln,
                              bias=consts[:, CEPS:CEPS + 1]))
    b.S(lambda: sc.activation(lnt[:, 3:4], lnt[:, 3:4], AF.Exp,
                              scale=consts[:, CNH:CNH + 1]))
    b.SC.wait(vc)
    b.V(lambda: vc.tensor_tensor(lnt[:, 4:5], lnt[:, 0:1], lnt[:, 3:4], ALU.mult))
    b.V(lambda: vc.tensor_scalar(dst[:, 0:D], src[:, 0:D], lnt[:, 3:4],
                                 lnt[:, 4:5], ALU.mult, ALU.subtract))


def build_program(cfg):
    nc = bass.Bass("TRN2", target_bir_lowering=False)
    b = B(nc, cfg)
    c = b.c
    S, D, H, KVH, Dh = c["S"], c["D"], c["H"], c["KVH"], c["Dh"]
    E, I, SHI = c["E"], c["I"], c["SHI"]
    QC, TCW, TC, TS = c["QC"], c["TCW"], c["TC"], c["TS"]
    kt_n, rep, G = c["kt_n"], c["rep"], c["G"]
    di_n, nQ, nK, do_n, DO = c["di_n"], c["nQ"], c["nK"], c["do_n"], c["DO"]
    n_qc, n_tcw = c["n_qc"], c["n_tcw"]
    ic_n, hc_n, n_tc, n_ts, tsub_n, nE = (
        c["ic_n"], c["hc_n"], c["n_tc"], c["n_ts"], c["tsub_n"], c["nE"])

    x_h = nc.dram_tensor("x", [S, D], F32, kind="ExternalInput").ap()
    wqkv_h = nc.dram_tensor("wqkv", [D, DO], F32, kind="ExternalInput").ap()
    bqkv_h = nc.dram_tensor("bqkv", [DO], F32, kind="ExternalInput").ap()
    scal_h = nc.dram_tensor("scaling", [Dh], F32, kind="ExternalInput").ap()
    wo_h = nc.dram_tensor("wo", [D, D], F32, kind="ExternalInput").ap()
    bo_h = nc.dram_tensor("bo", [D], F32, kind="ExternalInput").ap()
    wgate_h = nc.dram_tensor("wgate", [D, E], F32, kind="ExternalInput").ap()
    bgate_h = nc.dram_tensor("bgate", [E], F32, kind="ExternalInput").ap()
    wg_h = nc.dram_tensor("wg", [E, D, I], F32, kind="ExternalInput").ap()
    wu_h = nc.dram_tensor("wu", [E, D, I], F32, kind="ExternalInput").ap()
    wd_h = nc.dram_tensor("wd", [E, I, D], F32, kind="ExternalInput").ap()
    swg_h = nc.dram_tensor("swg", [D, SHI], F32, kind="ExternalInput").ap()
    swu_h = nc.dram_tensor("swu", [D, SHI], F32, kind="ExternalInput").ap()
    swd_h = nc.dram_tensor("swd", [SHI, D], F32, kind="ExternalInput").ap()
    out_h = nc.dram_tensor("out", [S, D], F32, kind="ExternalOutput").ap()
    x1s_h = nc.dram_tensor("x1s", [S, D], F32, kind="Internal").ap()
    x1b_h = nc.dram_tensor("x1b", [S, D], BF16, kind="Internal").ap()
    EWCOL0 = max((D // 128) * I, (I // 128) * D)
    nE0 = E + SHI // I
    ewb_h = nc.dram_tensor("ewb", [nE0, 3, 128, EWCOL0], BF16, kind="Internal").ap()

    ident = nc.alloc_sbuf_tensor("ident", [128, 128], F32)
    consts = nc.alloc_sbuf_tensor("consts", [128, 8], F32)
    CEPS, CNH, C1, CN1, CNC = 0, 1, 2, 3, 4
    bgate8 = nc.alloc_sbuf_tensor("bgate8", [E, 1], F32)
    ones1 = nc.alloc_sbuf_tensor("ones1", [1, 128], F32)
    ones1b = nc.alloc_sbuf_tensor("ones1b", [1, 128], BF16)
    attn_scope = ExitStack()
    _a = attn_scope.enter_context
    scal64 = _a(nc.sbuf_tensor("scal64", [64, 4], F32))
    scale128 = _a(nc.sbuf_tensor("scale128", [128, 1], F32))
    bq = _a(nc.sbuf_tensor("bq", [128, do_n], F32))
    beff = _a(nc.sbuf_tensor("beff", [128, do_n], F32))
    obB = _a(nc.sbuf_tensor("obB", [128, D], F32))
    QT = [_a(nc.sbuf_tensor(f"QT{p}", [128, S], BF16)) for p in range(nQ)]
    KTC = [_a(nc.sbuf_tensor(f"KTC{k}", [128, S], BF16)) for k in range(nK)]
    Vp = [_a(nc.sbuf_tensor(f"Vp{g}", [128, 128 * kt_n], BF16)) for g in range(G)]
    p0_scope = ExitStack()
    bo_row = p0_scope.enter_context(nc.sbuf_tensor("bo_row", [1, D], F32))

    psS = nc.alloc_psum_tensor("psS", [128, 2048], F32)
    psC = nc.alloc_psum_tensor("psC", [128, 2048], F32)

    class _PsView:

        def __init__(self, off):
            self.off = off

        def __getitem__(self, idx):
            ps_, fs = idx
            st = self.off + (fs.start or 0)
            sp = self.off + (fs.stop if fs.stop is not None else 1024)
            return psS[ps_, st:sp]

    psA = _PsView(0)
    psB = _PsView(1024)

    sy, tn, vc, sc, gp = nc.sync, nc.tensor, nc.vector, nc.scalar, nc.gpsimd
    TN, VC, SC, GP = b.TN, b.VC, b.SC, b.GP

    LD = Sem(nc, "LD", dma=True)
    WS = [Sem(nc, f"WS{i}", dma=True) for i in range(2)]
    X0 = [Sem(nc, f"X0{i}", dma=True) for i in range(4)]
    X0R = [Sem(nc, f"X0R{i}", dma=True) for i in range(2)]
    X1W = [Sem(nc, f"X1W{i}", dma=True) for i in range(2)]
    X1R = Sem(nc, "X1R", dma=True)
    X1B = [Sem(nc, f"X1B{i}", dma=True) for i in range(2)]
    XF = Sem(nc, "XF", dma=True)
    X0T = [Sem(nc, f"X0T{i}", dma=True) for i in range(4)]
    X1R2 = Sem(nc, "X1R2", dma=True)
    EW = [Sem(nc, f"EW{i}", dma=True) for i in range(2)]
    WR = Sem(nc, "WR", dma=True)
    EWI = [Sem(nc, f"EWI{i}", dma=True) for i in range(2)]
    EWO = Sem(nc, "EWO", dma=True)
    OUTS = [Sem(nc, f"OUTS{i}", dma=True) for i in range(2)]

    for k, v in {CEPS: LN_EPS, CNH: -0.5, C1: 1.0, CN1: -1.0, CNC: NEG_C}.items():
        b.V(lambda k=k, v=v: vc.memset(consts[:, k:k + 1], v))
    b.V(lambda: vc.memset(ident[:], 0.0))
    b.V(lambda: vc.memset(ones1[0:1, :], 1.0))
    b.V(lambda: vc.memset(ones1b[0:1, :], 1.0))
    VC.wait(gp)
    b.G(lambda: gp.affine_select(out=ident[:], in_=ident[:],
                                 compare_op=ALU.not_equal, fill=1.0, base=0,
                                 pattern=[[-1, 128]], channel_multiplier=1))
    for g in range(G):
        b.V(lambda g=g: vc.memset(Vp[g][:], 1.0))

    sy.dma_start(scal64[:, 0:1], scal_h.rearrange("(d a) -> d a", a=1)).then_inc(LD.h, 16)
    LD.n += 16
    for _c in range(do_n):
        sy.dma_start(bq[:, _c:_c + 1],
                     bqkv_h[_c * 128:(_c + 1) * 128].rearrange("(p a) -> p a", a=1),
                     allow_slow_non_contiguous=True).then_inc(LD.h, 16)
        LD.n += 16
    sy.dma_start(bo_row[0:1, :], bo_h.rearrange("(a d) -> a d", a=1)).then_inc(LD.h, 16)
    LD.n += 16
    sy.dma_start(bgate8[:, 0:1], bgate_h.rearrange("(e a) -> e a", a=1)).then_inc(LD.h, 16)
    LD.n += 16
    ld_p0 = LD.n

    LD.wait(vc, ld_p0)
    b.V(lambda: vc.tensor_scalar_mul(scal64[:, 1:2], scal64[:, 0:1], -1.0))
    b.V(lambda: vc.tensor_max(scal64[:, 1:2], scal64[:, 0:1], scal64[:, 1:2]))
    b.V(lambda: vc.tensor_single_scalar(scal64[:, 2:3], scal64[:, 0:1], 0.0, ALU.max))
    VC.wait(sc)
    b.S(lambda: sc.activation(scal64[:, 1:2], scal64[:, 1:2], AF.Exp,
                              scale=consts[0:64, CN1:CN1 + 1]))
    b.S(lambda: sc.activation(scal64[:, 1:2], scal64[:, 1:2], AF.Ln,
                              bias=consts[0:64, C1:C1 + 1]))
    SC.wait(vc)
    b.V(lambda: vc.tensor_add(scal64[:, 3:4], scal64[:, 2:3], scal64[:, 1:2]))
    b.V(lambda: vc.tensor_scalar_mul(scal64[:, 3:4], scal64[:, 3:4],
                                     R_SOFTPLUS_0 / math.sqrt(Dh)))
    b.V(lambda: vc.tensor_copy(scale128[0:64, 0:1], scal64[:, 3:4]))
    b.V(lambda: vc.tensor_copy(scale128[64:128, 0:1], scal64[:, 3:4]))
    LD.wait(vc, ld_p0)
    for dd in range(do_n):
        if dd < nQ:
            b.V(lambda dd=dd: vc.tensor_tensor(beff[:, dd:dd + 1], bq[:, dd:dd + 1],
                                               scale128[:, 0:1], ALU.mult))
        else:
            b.V(lambda dd=dd: vc.tensor_copy(beff[:, dd:dd + 1], bq[:, dd:dd + 1]))
    LD.wait(tn, ld_p0)
    VC.wait(tn)
    for m in range((D + 511) // 512):
        w = min(512, D - m * 512)
        TN.inc(tn.matmul(psC[0:128, m * 512:m * 512 + w], ones1[0:1, :],
                         bo_row[0:1, m * 512:m * 512 + w], start=True, stop=True))
    TN.wait(vc)
    b.V(lambda: vc.tensor_copy(obB[:, :], psC[:, 0:D]))

    nc.all_engine_barrier()
    p0_scope.close()

    ntiles = TCW // 128
    with ExitStack() as p1:
        wqkvb = p1.enter_context(nc.sbuf_tensor("wqkvb", [128, di_n * DO], BF16))
        stage = [p1.enter_context(nc.sbuf_tensor("wst0", [128, DO], F32))]
        x0b = [p1.enter_context(nc.sbuf_tensor(f"x0b{i}", [128, D], F32))
               for i in range(4)]
        x0c = [p1.enter_context(nc.sbuf_tensor(f"x0c{i}", [128, D], BF16))
               for i in range(4)]
        x0f = [p1.enter_context(nc.sbuf_tensor(f"x0f{i}", [128, di_n * TCW], BF16))
               for i in range(2)]
        vtmp = [p1.enter_context(nc.sbuf_tensor(f"vtmp{i}", [64, TCW], F32))
                for i in range(2)]

        for di in range(di_n):
            VC.wait(sy)
            sy.dma_start(stage[0][:], wqkv_h[di * 128:(di + 1) * 128, :]).then_inc(WS[0].h, 16)
            WS[0].n += 16
            WS[0].wait(vc)
            b.V(lambda di=di: vc.tensor_copy(wqkvb[:, di * DO:(di + 1) * DO],
                                             stage[0][:]))
        wqkv_ready = VC.n

        pslot_free = {}
        qkv_slot_free = {}
        vtmp_free = 0
        b._x0cast = {}
        b._x0t = {}
        b._qkv_last = {}
        for tc in range(n_tcw):
            for j in range(ntiles):
                s = (tc * ntiles + j) % 4
                VC.wait(sy, b._x0cast.get(s, 0))
                t0r = tc * TCW + j * 128
                sy.dma_start(x0b[s][:], x_h[t0r:t0r + 128, :]).then_inc(X0[s].h, 16)
                X0[s].n += 16
                X0[s].wait(vc)
                X0T[s].wait(vc, b._x0t.get(s, 0))
                b.V(lambda s=s: vc.tensor_copy(x0c[s][:, :], x0b[s][:, 0:D]))
                b._x0cast[s] = VC.n
                VC.wait(sy, b._x0cast[s])
                xsl = tc % 2
                TN.wait(sy, b._qkv_last.get(xsl, 0))
                X0T[s].wait(sy, b._x0t.get(s, 0))
                sy.dma_start_transpose(
                    bass.AP(x0f[xsl].ap().tensor, j * 128,
                            [[x0f[xsl].ap().ap[0][0], 128], [TCW, di_n], [1, 128]]),
                    x0c[s][:, :]).then_inc(X0T[s].h, 16)
                X0T[s].n += 16
                b._x0t[s] = X0T[s].n
            x0f_dma = tuple(X0T[i].n for i in range(4))
            xsl = tc % 2

            for dd in range(do_n):
                pst, off = [(psA, 0), (psA, 512), (psB, 0), (psB, 512)][dd % 4]
                pname = (id(pst), off)
                SC.wait(tn, qkv_slot_free.get(pname, 0))
                VC.wait(tn, wqkv_ready)
                for _i in range(4):
                    X0T[_i].wait(tn, x0f_dma[_i])
                for di in range(di_n):
                    ins = tn.matmul(pst[:, off:off + TCW],
                                    wqkvb[:, di * DO + dd * 128: di * DO + (dd + 1) * 128],
                                    x0f[xsl][:, di * TCW:(di + 1) * TCW],
                                    start=(di == 0), stop=(di == di_n - 1))
                TN.inc(ins)
                mm_done = TN.n
                TN.wait(sc, mm_done)
                if dd < nQ:
                    b.S(lambda dd=dd, tc=tc, pst=pst, off=off: sc.activation(
                        QT[dd][:, tc * TCW:(tc + 1) * TCW], pst[:, off:off + TCW],
                        AF.Identity, scale=scale128[:, 0:1], bias=beff[:, dd:dd + 1]))
                    qkv_slot_free[pname] = SC.n
                elif dd < nQ + nK:
                    kc = dd - nQ
                    cols = slice(tc * TCW, (tc + 1) * TCW)
                    b.S(lambda dd=dd, pst=pst, off=off, kc=kc, cols=cols:
                        sc.activation(KTC[kc][:, cols], pst[:, off:off + TCW],
                                      AF.Identity, bias=beff[:, dd:dd + 1]))
                    qkv_slot_free[pname] = SC.n
                else:
                    vcc = dd - nQ - nK
                    TN.wait(sc, vtmp_free)
                    b.S(lambda dd=dd, pst=pst, off=off: sc.activation(
                        vtmp[0][0:64, :], pst[0:64, off:off + TCW], AF.Identity,
                        bias=beff[0:64, dd:dd + 1]))
                    b.S(lambda dd=dd, pst=pst, off=off: sc.activation(
                        vtmp[1][0:64, :], pst[64:128, off:off + TCW], AF.Identity,
                        bias=beff[64:128, dd:dd + 1]))
                    qkv_slot_free[pname] = SC.n
                    vevac = SC.n
                    for ktl in range(ntiles):
                        for gl in range(2):
                            g = 2 * vcc + gl
                            kt = tc * ntiles + ktl
                            sl = (ktl * 2 + gl) % 4
                            co = sl * 512
                            SC.wait(tn, vevac)
                            VC.wait(tn, pslot_free.get(sl, 0))
                            TN.inc(tn.transpose(psC[:, co:co + 64],
                                                vtmp[gl][0:64, ktl * 128:(ktl + 1) * 128],
                                                ident[0:64, 0:64]))
                            TN.wait(vc)
                            vdo = 128 * kt + (64 if g % 2 else 0)
                            b.V(lambda g=g, vdo=vdo, co=co: vc.tensor_copy(
                                Vp[g][:, vdo:vdo + 64],
                                psC[:, co:co + 64]))
                            pslot_free[sl] = VC.n
                    vtmp_free = TN.n
            b._qkv_last[xsl] = TN.n

    X0[0].wait(sy)
    X0[1].wait(sy)
    WS[0].wait(sy)
    nc.all_engine_barrier()

    with ExitStack() as p2:
        attnF = [p2.enter_context(nc.sbuf_tensor(f"attnF{p}", [128, QC], BF16))
                 for p in range(nQ)]
        woB = p2.enter_context(nc.sbuf_tensor("woB", [128, di_n * D], BF16))
        pb = [p2.enter_context(nc.sbuf_tensor(f"pb{i}", [128, 2 * QC], BF16))
              for i in range(4)]
        mrgP = p2.enter_context(nc.sbuf_tensor("mrgP", [128, QC], F32))
        mrgS = p2.enter_context(nc.sbuf_tensor("mrgS", [128, QC], F32))
        mrgR = p2.enter_context(nc.sbuf_tensor("mrgR", [128, QC], F32))
        CW0 = min(1024, max(hc_n * I, ic_n * D))
        cstage = [p2.enter_context(nc.sbuf_tensor("cstage0", [128, CW0], F32))]
        x1bb = [p2.enter_context(nc.sbuf_tensor(f"x1bb{i}", [128, D], BF16))
                for i in range(2)]
        x0r = [p2.enter_context(nc.sbuf_tensor(f"x0r{i}", [128, D], F32))
               for i in range(2)]
        r1_all = p2.enter_context(nc.sbuf_tensor("r1_all", [128, (QC // 128) * D], F32))
        x1t = [p2.enter_context(nc.sbuf_tensor(f"x1t{i}", [128, D], F32))
               for i in range(2)]
        lnt8 = p2.enter_context(nc.sbuf_tensor("lnt8", [128, 64], F32))
        ewcast = p2.enter_context(nc.sbuf_tensor("ewcast", [128, CW0], BF16))

        for di in range(di_n):
            VC.wait(sy)
            sy.dma_start(cstage[0][:, 0:D],
                         wo_h[di * 128:(di + 1) * 128, :]).then_inc(WS[0].h, 16)
            WS[0].n += 16
            WS[0].wait(vc)
            b.V(lambda di=di: vc.tensor_copy(woB[:, di * D:(di + 1) * D],
                                             cstage[0][:, 0:D]))
        wo_ready = VC.n
        VC_after_wo = VC.n

        EWCOL = max(hc_n * I, ic_n * D)
        CW = min(1024, EWCOL)
        cast_jobs = []
        for e in range(nE):
            if e < E:
                srcs = [wg_h[e], wu_h[e], wd_h[e]]
            else:
                hh = e - E
                srcs = [swg_h[:, hh * I:(hh + 1) * I],
                        swu_h[:, hh * I:(hh + 1) * I],
                        swd_h[hh * I:(hh + 1) * I, :]]
            for m in range(3):
                ncols = I if m < 2 else D
                cpc = CW // ncols
                for k in range(EWCOL // CW):
                    cast_jobs.append((e, m, k, srcs[m], ncols, cpc))
        b._cast_issue = 0
        b._cast_done = 0
        b._cast_gp = {}

        def issue_cast_dma():
            if b._cast_issue >= len(cast_jobs):
                return
            i = b._cast_issue
            b._cast_issue += 1
            e, m, k, src, ncols, cpc = cast_jobs[i]
            sl = 0
            if i == 0:
                VC.wait(sy, VC_after_wo)
            GP.wait(sy, b._cast_gp.get(sl, 0))
            sy.dma_start(
                cstage[sl].ap().rearrange("p (c i) -> p c i", c=cpc)[:, :, 0:ncols],
                src.rearrange("(c p) i -> p c i", p=128)[:, k * cpc:(k + 1) * cpc, :],
            ).then_inc(EWI[sl].h, 16)
            EWI[sl].n += 16

        def emit_cast_chunk():
            if b._cast_done >= b._cast_issue:
                return
            i = b._cast_done
            b._cast_done += 1
            e, m, k, src, ncols, cpc = cast_jobs[i]
            sl = 0
            EWI[sl].wait(gp, (i + 1) * 16)
            EWO.wait(gp)
            b.G(lambda sl=sl, n=cpc * ncols: gp.tensor_copy(
                ewcast[:, 0:n], cstage[sl][:, 0:n]))
            b._cast_gp[sl] = GP.n
            GP.wait(sy, b._cast_gp[sl])
            sy.dma_start(ewb_h[e, m, :, k * CW:(k + 1) * CW],
                         ewcast[:, 0:CW]).then_inc(EWO.h, 16)
            EWO.n += 16
        issue_cast_dma()

        b._expslot = {}
        b._pbslot = {}
        b._psC_free_v = 0
        b._oproj_tn = 0
        b._attn_div_v = 0
        b._x1st = {}
        b._x1bt = {}
        b._fin_v = {}
        oslot_free = {}
        pending_fin = None

        def emit_fin(st):
            p, pvlast, first_of_qc = st
            TN.wait(vc, pvlast)
            if first_of_qc:
                TN.wait(vc, b._oproj_tn)
            b.V(lambda: vc.tensor_copy(mrgP[0:64, :], psC[0:64, 0:QC]))
            b.V(lambda: vc.tensor_add(mrgP[0:64, :], mrgP[0:64, :],
                                      psC[0:64, 512:512 + QC]))
            b.V(lambda: vc.tensor_copy(mrgS[64:128, :], psC[64:128, 0:QC]))
            b.V(lambda: vc.tensor_add(mrgS[64:128, :], mrgS[64:128, :],
                                      psC[64:128, 512:512 + QC]))
            b.V(lambda: vc.tensor_copy(mrgP[64:128, :], psC[64:128, 1024:1024 + QC]))
            b.V(lambda: vc.tensor_add(mrgP[64:128, :], mrgP[64:128, :],
                                      psC[64:128, 1536:1536 + QC]))
            b.V(lambda: vc.tensor_copy(mrgS[0:64, :], psC[0:64, 1024:1024 + QC]))
            b.V(lambda: vc.tensor_add(mrgS[0:64, :], mrgS[0:64, :],
                                      psC[0:64, 1536:1536 + QC]))
            b._psC_free_v = VC.n
            b.V(lambda: vc.reciprocal(mrgR[0:64, :], mrgS[64:128, :]))
            b.V(lambda: vc.reciprocal(mrgR[64:128, :], mrgS[0:64, :]))
            b.V(lambda p=p: vc.tensor_tensor(
                attnF[p][0:64, :], mrgP[0:64, :], mrgR[0:64, :], ALU.mult))
            b.V(lambda p=p: vc.tensor_tensor(
                attnF[p][64:128, :], mrgP[64:128, :], mrgR[64:128, :], ALU.mult))
            b._attn_div_v = VC.n
            b._fin_v[p] = VC.n

        pending_tail = None
        ntile = QC // 128
        nbs = (D + 511) // 512
        mvo = 6 * nbs * ntile

        def emit_ln_tail(q0t):
            for j in range(ntile):
                b.V(lambda j=j: vc.bn_aggr(
                    lnt8[:, mvo + 2 * j:mvo + 2 * j + 2],
                    lnt8[:, 6 * nbs * j:6 * nbs * (j + 1)]))
            me = _ap2(lnt8, mvo, [[2, ntile]])
            va = _ap2(lnt8, mvo + 1, [[2, ntile]])
            nm = lnt8[:, mvo + 2 * ntile:mvo + 3 * ntile]
            VC.wait(sc)
            b.S(lambda: sc.activation(va, va, AF.Ln, bias=consts[:, 0:1]))
            b.S(lambda: sc.activation(va, va, AF.Exp, scale=consts[:, 1:2]))
            SC.wait(vc)
            b.V(lambda: vc.tensor_tensor(nm, me, va, ALU.mult))
            for j in range(ntile):
                x1 = x1t[j % 2]
                X1W[j % 2].wait(vc, b._x1st.get(j % 2, 0))
                b.V(lambda j=j, x1=x1: vc.tensor_scalar(
                    x1[:, 0:D], r1_all[:, j * D:(j + 1) * D],
                    lnt8[:, mvo + 1 + 2 * j:mvo + 2 + 2 * j], nm[:, j:j + 1],
                    ALU.mult, ALU.subtract))
                X1B[j % 2].wait(vc, b._x1bt.get(j % 2, 0))
                b.V(lambda j=j, x1=x1: vc.tensor_copy(x1bb[j % 2][:, :],
                                                      x1[:, 0:D]))
                VC.wait(sy)
                rows = slice(q0t + j * 128, q0t + (j + 1) * 128)
                sy.dma_start(x1s_h[rows, :], x1[:]).then_inc(X1W[j % 2].h, 16)
                X1W[j % 2].n += 16
                sy.dma_start(x1b_h[rows, :], x1bb[j % 2][:, :]).then_inc(X1B[j % 2].h, 16)
                X1B[j % 2].n += 16
                b._x1st[j % 2] = X1W[j % 2].n
                b._x1bt[j % 2] = X1B[j % 2].n

        for qc in range(n_qc):
            q0 = qc * QC
            NB = nQ * kt_n
            exp_val = {}
            scores_done = {}
            for v in range(NB + 4):
                if v >= 4:
                    u = v - 4
                    pP, ktP = divmod(u, kt_n)
                    if ktP == 0 and pending_fin is not None:
                        emit_fin(pending_fin)
                        pending_fin = None
                    kcP = pP // rep
                    gE, gO = 2 * kcP, 2 * kcP + 1
                    slp = u % 4
                    vo = ktP * 128
                    st = (ktP == 0)
                    sp = (ktP == kt_n - 1)
                    SC.wait(tn, exp_val[u])
                    if st:
                        VC.wait(tn, b._psC_free_v)
                    tn.matmul(psC[0:128, 0:QC], Vp[gE][0:64, vo:vo + 128],
                              pb[slp][0:64, 0:QC], start=st, stop=sp,
                              tile_position=(0, 0))
                    tn.matmul(psC[0:128, 512:512 + QC],
                              Vp[gE][64:128, vo:vo + 128],
                              pb[slp][64:128, 0:QC], start=st, stop=sp,
                              tile_position=(64, 0))
                    tn.matmul(psC[0:128, 1024:1024 + QC],
                              Vp[gO][0:64, vo:vo + 128],
                              pb[slp][0:64, QC:2 * QC], start=st, stop=sp,
                              tile_position=(0, 0))
                    TN.inc(tn.matmul(psC[0:128, 1536:1536 + QC],
                                     Vp[gO][64:128, vo:vo + 128],
                                     pb[slp][64:128, QC:2 * QC], start=st,
                                     stop=sp, tile_position=(64, 0)))
                    b._pbslot[slp] = TN.n
                    if sp:
                        pending_fin = (pP, TN.n, pP == 0)
                if v < NB:
                    pS, ktS = divmod(v, kt_n)
                    sl = v % 2
                    so = sl * 1024
                    ko = ktS * 128
                    kcS = pS // rep
                    SC.wait(tn, b._expslot.get(sl, 0))
                    tn.matmul(psS[0:128, so:so + QC],
                              KTC[kcS][0:64, ko:ko + 128],
                              QT[pS][0:64, q0:q0 + QC],
                              start=True, stop=True, tile_position=(0, 0))
                    TN.inc(tn.matmul(psS[0:128, so + QC:so + 2 * QC],
                                     KTC[kcS][64:128, ko:ko + 128],
                                     QT[pS][64:128, q0:q0 + QC],
                                     start=True, stop=True,
                                     tile_position=(64, 0)))
                    scores_done[v] = TN.n
                if 1 <= v <= NB:
                    ke = v - 1
                    sl = ke % 2
                    TN.wait(sc, scores_done[ke])
                    TN.wait(sc, b._pbslot.get(ke % 4, 0))
                    SC.inc(sc.activation(pb[ke % 4][:, 0:2 * QC],
                                         psS[:, sl * 1024:sl * 1024 + 2 * QC],
                                         AF.Exp, bias=consts[:, CNC:CNC + 1]))
                    exp_val[ke] = SC.n
                    b._expslot[sl] = SC.n
                if v == 6 and pending_tail is not None:
                    emit_ln_tail(pending_tail)
                    pending_tail = None
                if v % 8 == 4:
                    emit_cast_chunk()
                    issue_cast_dma()
            if pending_fin is not None:
                emit_fin(pending_fin)
                pending_fin = None

            oNC = (D + 511) // 512
            for j in range(ntile):
                sq = j % 2
                po = 1024 * sq
                rows = slice(q0 + j * 128, q0 + (j + 1) * 128)
                VC.wait(sy, oslot_free.get(("x", sq), 0))
                sy.dma_start(x0r[sq][:, 0:D], x_h[rows, :]).then_inc(X0R[sq].h, 16)
                X0R[sq].n += 16
                VC.wait(tn, max(wo_ready, b._psC_free_v,
                                oslot_free.get(("ps", sq), 0)))
                for m in range(oNC):
                    w = min(512, D - m * 512)
                    for di in range(di_n):
                        VC.wait(tn, b._fin_v.get(di, 0))
                        ins = tn.matmul(psC[0:128, po + m * 512:po + m * 512 + w],
                                        attnF[di][:, j * 128:(j + 1) * 128],
                                        woB[:, di * D + m * 512: di * D + m * 512 + w],
                                        start=(di == 0), stop=(di == di_n - 1))
                TN.inc(ins)
                odone = TN.n
                b._oproj_tn = TN.n
                TN.wait(vc, odone)
                X0R[sq].wait(vc)
                dst = r1_all[:, j * D:(j + 1) * D]
                b.V(lambda po=po, sq=sq, dst=dst: vc.tensor_add(
                    dst, psC[0:128, po:po + D], x0r[sq][:, 0:D]))
                b.V(lambda dst=dst: vc.tensor_add(dst, dst, obB[:, :]))
                oslot_free[("x", sq)] = VC.n
                oslot_free[("ps", sq)] = VC.n
                nbs = (D + 511) // 512
                for c in range(nbs):
                    w = min(512, D - c * 512)
                    b.V(lambda j=j, c=c, w=w, dst=dst, nbs=nbs: vc.bn_stats(
                        lnt8[:, 6 * (nbs * j + c):6 * (nbs * j + c) + 6],
                        dst[:, c * 512:c * 512 + w]))
            b._psC_free_v = VC.n
            pending_tail = q0

        if pending_tail is not None:
            emit_ln_tail(pending_tail)
            pending_tail = None
        while b._cast_done < len(cast_jobs):
            emit_cast_chunk()
            issue_cast_dma()
        X0R[0].wait(sy)
        X0R[1].wait(sy)
        X1B[0].wait(sy)
        X1B[1].wait(sy)
        WS[0].wait(sy)
        EWI[0].wait(sy)
        EWI[1].wait(sy)
        EWO.wait(sy)

    nc.all_engine_barrier()
    attn_scope.close()

    with ExitStack() as p3:
        x1c = [p3.enter_context(nc.sbuf_tensor(f"x1c{i}", [128, tsub_n * D], F32))
               for i in range(2)]
        x1f = p3.enter_context(nc.sbuf_tensor("x1f", [128, di_n * TC], BF16))
        wb = p3.enter_context(nc.sbuf_tensor("wb", [128, E * TC], BF16))
        y_acc = p3.enter_context(nc.sbuf_tensor("y_acc", [128, hc_n * TC], F32))
        gs = [p3.enter_context(nc.sbuf_tensor(f"gs{i}", [128, ic_n * TS], BF16))
              for i in range(2)]
        guf = p3.enter_context(nc.sbuf_tensor("guf", [128, ic_n * TS], BF16))
        gu = [p3.enter_context(nc.sbuf_tensor(f"gu{i}", [128, ic_n * TS], BF16))
              for i in range(2)]
        wgateB = p3.enter_context(nc.sbuf_tensor("wgateB", [128, di_n * E], BF16))
        wgstage = p3.enter_context(nc.sbuf_tensor("wgstage", [128, di_n * E], F32))
        nsub = S // 128
        wFrow = p3.enter_context(nc.sbuf_tensor("wFrow", [E, S], BF16))
        wrowB1 = p3.enter_context(nc.sbuf_tensor("wrowB1", [1, TC], BF16))
        lnt38 = p3.enter_context(nc.sbuf_tensor("lnt38", [128, 128], F32))
        pg = ExitStack()
        _g = pg.enter_context
        sfc_f = _g(nc.sbuf_tensor("sfc_f", [E, TC], F32))
        st_t = _g(nc.sbuf_tensor("st_t", [128, nsub * E], F32))
        gsum = _g(nc.sbuf_tensor("gsum", [128, nsub * 4], F32))
        cntg = _g(nc.sbuf_tensor("cntg", [128, nsub * 4], F32))
        selg = _g(nc.sbuf_tensor("selg", [128, nsub * 4], F32))
        tmpe = _g(nc.sbuf_tensor("tmpe", [128, nsub * E], F32))
        cnte = _g(nc.sbuf_tensor("cnte", [128, nsub * E], F32))
        wraw = _g(nc.sbuf_tensor("wraw", [128, nsub * E], F32))
        scr8 = _g(nc.sbuf_tensor("scr8", [128, nsub * E], F32))
        den = _g(nc.sbuf_tensor("den", [128, nsub], F32))

        sy.dma_start(wgstage.ap().rearrange("p (c e) -> p c e", c=di_n),
                     wgate_h.rearrange("(c p) e -> p c e", p=128)).then_inc(LD.h, 16)
        LD.n += 16
        LD.wait(vc)
        b.V(lambda: vc.tensor_copy(wgateB[:, :], wgstage[:, :]))
        wgate_ready = VC.n

        pslot_free = {}
        b._gate_sc = 0
        for tcg in list(range(1, n_tc)) + [0]:
            tg0 = tcg * TC
            X1B[0].wait(sy)
            X1B[1].wait(sy)
            XF.wait(sy)
            TN.wait(sy)
            sy.dma_start_transpose(
                bass.AP(x1f.ap().tensor, 0,
                        [[x1f.ap().ap[0][0], 128], [TC, di_n], [1, TC]]),
                x1b_h[tg0:tg0 + TC, :]).then_inc(XF.h, 16)
            XF.n += 16
            b._xf_val = XF.n
            XF.wait(tn, b._xf_val)
            VC.wait(tn, wgate_ready)
            SC.wait(tn)
            for di in range(di_n):
                for m in range((TC + 511) // 512):
                    w = min(512, TC - m * 512)
                    ins = tn.matmul(psA[0:E, m * 512:m * 512 + w],
                                    wgateB[:, di * E:(di + 1) * E],
                                    x1f[:, di * TC + m * 512: di * TC + m * 512 + w],
                                    start=(di == 0), stop=(di == di_n - 1))
            TN.inc(ins)
            TN.wait(sc)
            b.S(lambda: sc.activation(sfc_f[:, :], psA[0:E, 0:TC], AF.Sigmoid))
            b._gate_sc = SC.n
            SC.wait(vc)
            LD.wait(vc)
            b.V(lambda: vc.tensor_scalar_add(sfc_f[:, :], sfc_f[:, :], bgate8[:, 0:1]))
            sfc_ready = VC.n
            for j in range(tsub_n):
                jj = tcg * tsub_n + j
                sl = jj % 4
                co = sl * 512
                VC.wait(tn, max(pslot_free.get(sl, 0), sfc_ready))
                TN.inc(tn.transpose(psC[0:128, co:co + E],
                                    sfc_f[:, j * 128:(j + 1) * 128],
                                    ident[0:E, 0:E]))
                TN.wait(vc)
                b.V(lambda jj=jj, co=co: vc.tensor_copy(
                    st_t[:, jj * E:(jj + 1) * E], psC[:, co:co + E]))
                pslot_free[sl] = VC.n

        def sl_e(tens, e, n=E):
            return _ap2(tens, e, [[n, nsub]])

        for g4 in range(4):
            b.V(lambda g4=g4: vc.tensor_tensor(sl_e(gsum, g4, 4),
                                               sl_e(st_t, 2 * g4),
                                               sl_e(st_t, 2 * g4 + 1), ALU.add))
        for g4 in range(4):
            others = [x for x in range(4) if x != g4]
            b.V(lambda g4=g4, o=others[0]: vc.tensor_tensor(
                sl_e(cntg, g4, 4), sl_e(gsum, o, 4), sl_e(gsum, g4, 4), ALU.is_gt))
            for o in others[1:]:
                b.V(lambda g4=g4, o=o: vc.tensor_tensor(
                    sl_e(selg, g4, 4), sl_e(gsum, o, 4), sl_e(gsum, g4, 4), ALU.is_gt))
                b.V(lambda g4=g4: vc.tensor_tensor(
                    sl_e(cntg, g4, 4), sl_e(cntg, g4, 4), sl_e(selg, g4, 4), ALU.add))
        for g4 in range(4):
            b.V(lambda g4=g4: vc.tensor_single_scalar(
                sl_e(selg, g4, 4), sl_e(cntg, g4, 4), 2.0, ALU.is_lt))
        for e in range(E):
            b.V(lambda e=e: vc.tensor_tensor(
                sl_e(tmpe, e), sl_e(st_t, e), sl_e(selg, e // 2, 4), ALU.mult))
        for e in range(E):
            others = [x for x in range(E) if x != e]
            b.V(lambda e=e, o=others[0]: vc.tensor_tensor(
                sl_e(cnte, e), sl_e(tmpe, o), sl_e(tmpe, e), ALU.is_gt))
            for o in others[1:]:
                b.V(lambda e=e, o=o: vc.tensor_tensor(
                    sl_e(scr8, e), sl_e(tmpe, o), sl_e(tmpe, e), ALU.is_gt))
                b.V(lambda e=e: vc.tensor_tensor(
                    sl_e(cnte, e), sl_e(cnte, e), sl_e(scr8, e), ALU.add))
        for e in range(E):
            b.V(lambda e=e: vc.tensor_single_scalar(
                sl_e(scr8, e), sl_e(cnte, e), 2.0, ALU.is_lt))
            b.V(lambda e=e: vc.tensor_single_scalar(
                sl_e(cnte, e), sl_e(tmpe, e), 0.0, ALU.is_gt))
            b.V(lambda e=e: vc.tensor_tensor(
                sl_e(scr8, e), sl_e(scr8, e), sl_e(cnte, e), ALU.mult))
            b.V(lambda e=e: vc.tensor_tensor(
                sl_e(wraw, e), sl_e(scr8, e), sl_e(st_t, e), ALU.mult))
        den_ap = _ap2(den, 0, [[1, nsub]])
        b.V(lambda: vc.tensor_tensor(den_ap, sl_e(wraw, 0), sl_e(wraw, 1), ALU.add))
        for e in range(2, E):
            b.V(lambda e=e: vc.tensor_tensor(den_ap, den_ap, sl_e(wraw, e), ALU.add))
        b.V(lambda: vc.tensor_scalar_add(den_ap, den_ap, 1e-20))
        b.V(lambda: vc.reciprocal(den_ap, den_ap))
        b.V(lambda: vc.tensor_scalar_mul(den_ap, den_ap, ROUTED_SCALE))
        for e in range(E):
            b.V(lambda e=e: vc.tensor_tensor(
                sl_e(wraw, e), sl_e(wraw, e), den_ap, ALU.mult))
        w_tok = VC.n
        for jj in range(nsub):
            sl = jj % 4
            co = sl * 512
            VC.wait(tn, max(pslot_free.get(sl, 0), w_tok))
            TN.inc(tn.transpose(psC[0:E, co:co + 128],
                                wraw[:, jj * E:(jj + 1) * E], ident[:]))
            TN.wait(vc)
            b.V(lambda jj=jj, co=co: vc.tensor_copy(
                wFrow[:, jj * 128:(jj + 1) * 128], psC[0:E, co:co + 128]))
            pslot_free[sl] = VC.n
        wfrow_done = VC.n
        if not hasattr(b, "_slot_last_mm"):
            b._slot_last_mm = {}
        b._slot_last_mm[0] = TN.n
        b._slot_last_mm[1] = TN.n
        pg.close()
        EWCOL = max(hc_n * I, ic_n * D)
        ew = [[p3.enter_context(nc.sbuf_tensor(f"ew{s}{m}", [128, EWCOL], BF16))
               for m in range(3)] for s in range(2)]

        est_free = {}
        ew_ready = {}

        def prefetch_expert(e, s):
            EWCOL = max(hc_n * I, ic_n * D)
            EWO.wait(sy)
            EW[s].wait(sy)
            VC.wait(sy, wfrow_done)
            TN.wait(sy, b._slot_last_mm.get(s, 0))
            for m in range(3):
                sy.dma_start(ew[s][m][:, 0:EWCOL],
                             ewb_h[e, m, :, 0:EWCOL]).then_inc(EW[s].h, 16)
                EW[s].n += 16
            ew_ready[s] = EW[s].n

        b._wb_mm = 0
        fin_read = {}
        x2_store = {}
        half = max(1, tsub_n // 2)
        b._x1dma = {}
        b._x1c_read = {}

        def emit_x1load(tc):
            slx = tc % 2
            t0l = tc * TC
            VC.wait(sy, b._x1c_read.get(slx, 0))
            X1W[0].wait(sy)
            X1W[1].wait(sy)
            X1R.wait(sy)
            X1R2.wait(sy)
            sy.dma_start(
                _ap2(x1c[slx], 0, [[D, half], [1, D]]),
                x1s_h[t0l:t0l + half * 128, :].rearrange("(j p) f -> p j f", p=128),
            ).then_inc(X1R.h, 16)
            X1R.n += 16
            if tsub_n > half:
                sy.dma_start(
                    _ap2(x1c[slx], half * D, [[D, tsub_n - half], [1, D]]),
                    x1s_h[t0l + half * 128:t0l + TC, :].rearrange(
                        "(j p) f -> p j f", p=128),
                ).then_inc(X1R2.h, 16)
                X1R2.n += 16
            b._x1dma[tc] = (X1R.n, X1R2.n)

        emit_x1load(0)
        b._prev_aval = 0
        for tc in range(n_tc):
            b.soft_barrier()
            t0 = tc * TC
            slx = tc % 2
            d1, d2 = b._x1dma[tc]
            if tc + 1 < n_tc:
                emit_x1load(tc + 1)

            if tc > 0:
                X1W[0].wait(sy)
                X1W[1].wait(sy)
                X1B[0].wait(sy)
                X1B[1].wait(sy)
                TN.wait(sy, b._prev_aval)
                XF.wait(sy)
                sy.dma_start_transpose(
                    bass.AP(x1f.ap().tensor, 0,
                            [[x1f.ap().ap[0][0], 128], [TC, di_n], [1, TC]]),
                    x1b_h[t0:t0 + TC, :]).then_inc(XF.h, 16)
                XF.n += 16
                b._xf_val = XF.n

            NU = nE * n_ts
            if not hasattr(b, "_slot_last_mm"):
                b._slot_last_mm = {}
            silu_val = {}
            gu_ready = {}
            ydone = {}
            aval_gate = {}
            aval_up = {}
            cval = {}
            if tc == 0:
                prefetch_expert(0, 0)
            prefetch_expert(1, 1)

            def emit_A(u):
                e, ts = divmod(u, n_ts)
                sl_w = (e % 2)
                tso = ts * TS
                wgB, wuB, _ = ew[sl_w]
                if u > 0:
                    SC.wait(tn, silu_val[u - 1])
                    VC.wait(tn, gu_ready[u - 1])
                else:
                    SC.wait(tn)
                EW[sl_w].wait(tn, ew_ready[sl_w])
                XF.wait(tn, b._xf_val)
                for ic in range(ic_n):
                    for hcx in range(hc_n):
                        ins = tn.matmul(psA[:, ic * TS:(ic + 1) * TS],
                                        wgB[:, hcx * I + ic * 128: hcx * I + (ic + 1) * 128],
                                        x1f[:, hcx * TC + tso: hcx * TC + tso + TS],
                                        start=(hcx == 0), stop=(hcx == hc_n - 1))
                TN.inc(ins)
                aval_gate[u] = TN.n
                for ic in range(ic_n):
                    for hcx in range(hc_n):
                        ins = tn.matmul(psB[:, ic * TS:(ic + 1) * TS],
                                        wuB[:, hcx * I + ic * 128: hcx * I + (ic + 1) * 128],
                                        x1f[:, hcx * TC + tso: hcx * TC + tso + TS],
                                        start=(hcx == 0), stop=(hcx == hc_n - 1))
                TN.inc(ins)
                aval_up[u] = TN.n

            def emit_B(u):
                e, ts = divmod(u, n_ts)
                tso = ts * TS
                p_ = u % 2
                TN.wait(sc, aval_gate[u])
                if u >= 2:
                    VC.wait(sc, gu_ready[u - 2])
                b.S(lambda p_=p_, n=ic_n * TS: sc.activation(
                    gs[p_][:, 0:n], psA[:, 0:n], AF.Silu))
                silu_val[u] = SC.n
                TN.wait(vc, aval_up[u])
                if e < E:
                    VC.wait(vc, wb_ready)
                    for ic in range(ic_n):
                        b.V(lambda ic=ic, e=e, tso=tso: vc.tensor_tensor(
                            guf[:, ic * TS:(ic + 1) * TS],
                            psB[:, ic * TS:(ic + 1) * TS],
                            wb[:, e * TC + tso: e * TC + tso + TS], ALU.mult))
                    SC.wait(vc, silu_val[u])
                    b.V(lambda p_=p_, n=ic_n * TS: vc.tensor_tensor(
                        gu[p_][:, 0:n], gs[p_][:, 0:n], guf[:, 0:n], ALU.mult))
                else:
                    SC.wait(vc, silu_val[u])
                    b.V(lambda p_=p_, n=ic_n * TS: vc.tensor_tensor(
                        gu[p_][:, 0:n], gs[p_][:, 0:n], psB[:, 0:n], ALU.mult))
                gu_ready[u] = VC.n

            def emit_C(u):
                e, ts = divmod(u, n_ts)
                sl_w = (e % 2)
                _, _, wdB = ew[sl_w]
                p_ = u % 2
                VC.wait(tn, gu_ready[u])
                if u >= 1:
                    VC.wait(tn, ydone[u - 1])
                for hcx in range(hc_n):
                    for ic in range(ic_n):
                        ins = tn.matmul(psC[:, hcx * TS:(hcx + 1) * TS],
                                        wdB[:, ic * D + hcx * 128: ic * D + (hcx + 1) * 128],
                                        gu[p_][:, ic * TS:(ic + 1) * TS],
                                        start=(ic == 0), stop=(ic == ic_n - 1))
                TN.inc(ins)
                cval[u] = TN.n
                b._slot_last_mm[sl_w] = TN.n

            def emit_D(u):
                e, ts = divmod(u, n_ts)
                tso = ts * TS
                TN.wait(vc, cval[u])
                if u < n_ts:
                    OUTS[0].wait(vc, x2_store.get(0, 0))
                    OUTS[1].wait(vc, x2_store.get(1, 0))
                yv = bass.AP(y_acc.ap().tensor, tso,
                             [[y_acc.ap().ap[0][0], 128], [TC, hc_n], [1, TS]])
                pv = bass.AP(psC.ap().tensor, 0,
                             [[psC.ap().ap[0][0], 128], [TS, hc_n], [1, TS]])
                if e == 0:
                    b.V(lambda: vc.tensor_copy(yv, pv))
                else:
                    b.V(lambda: vc.tensor_tensor(yv, yv, pv, ALU.add))
                ydone[u] = VC.n

            emit_A(0)
            for e in range(E):
                VC.wait(sy, wfrow_done)
                TN.wait(sy, b._wb_mm)
                sy.dma_start(wrowB1[0:1, :],
                             wFrow[e:e + 1, t0:t0 + TC]).then_inc(WR.h, 16)
                WR.n += 16
                WR.wait(tn)
                VC.wait(tn)
                for m in range((TC + 511) // 512):
                    w = min(512, TC - m * 512)
                    TN.inc(tn.matmul(psC[0:128, m * 512:m * 512 + w],
                                     ones1b[0:1, :],
                                     wrowB1[0:1, m * 512:m * 512 + w],
                                     start=True, stop=True))
                b._wb_mm = TN.n
                TN.wait(vc)
                b.V(lambda e=e: vc.tensor_copy(wb[:, e * TC:(e + 1) * TC],
                                               psC[:, 0:TC]))
            wb_ready = VC.n
            emit_B(0)

            for u in range(1, NU + 1):
                if u < NU:
                    emit_A(u)
                    emit_B(u)
                emit_C(u - 1)
                emit_D(u - 1)
                if u % n_ts == 0:
                    e_next = u // n_ts + 1
                    if e_next < nE:
                        prefetch_expert(e_next, e_next % 2)
                    elif tc + 1 < n_tc:
                        prefetch_expert(0, 0)
            ydone_v = ydone[NU - 1]
            b._prev_aval = aval_up[NU - 1]

            ntile3 = tsub_n
            r2_done = {}
            for j in range(ntile3):
                pst = psA if j % 2 == 0 else psB
                if j == 0:
                    VC.wait(tn, ydone_v)
                    SC.wait(tn)
                    VC.wait(tn)
                else:
                    VC.wait(tn, fin_read.get(j % 2, 0))
                for hcx in range(hc_n):
                    ins = tn.transpose(pst[:, hcx * 128:(hcx + 1) * 128],
                                       y_acc[:, hcx * TC + j * 128: hcx * TC + (j + 1) * 128],
                                       ident[:])
                TN.inc(ins)
                tdone = TN.n
                TN.wait(vc, tdone)
                X1R.wait(vc, d1)
                X1R2.wait(vc, d2)
                dst = x1c[slx][:, j * D:(j + 1) * D]
                b.V(lambda pst=pst, dst=dst: vc.tensor_add(dst, dst, pst[:, 0:D]))
                fin_read[j % 2] = VC.n
                r2_done[j] = VC.n
                nbs3 = (D + 511) // 512
                for c in range(nbs3):
                    w = min(512, D - c * 512)
                    b.V(lambda j=j, c=c, w=w, dst=dst, nbs3=nbs3: vc.bn_stats(
                        lnt38[:, 6 * (nbs3 * j + c):6 * (nbs3 * j + c) + 6],
                        dst[:, c * 512:c * 512 + w]))
            nbs3 = (D + 511) // 512
            mvo3 = 6 * nbs3 * ntile3
            for j in range(ntile3):
                b.V(lambda j=j, nbs3=nbs3, mvo3=mvo3: vc.bn_aggr(
                    lnt38[:, mvo3 + 2 * j:mvo3 + 2 * j + 2],
                    lnt38[:, 6 * nbs3 * j:6 * nbs3 * (j + 1)]))
            me3 = _ap2(lnt38, mvo3, [[2, ntile3]])
            va3 = _ap2(lnt38, mvo3 + 1, [[2, ntile3]])
            nm3 = lnt38[:, mvo3 + 2 * ntile3:mvo3 + 3 * ntile3]
            VC.wait(sc)
            b.S(lambda: sc.activation(va3, va3, AF.Ln, bias=consts[:, 0:1]))
            b.S(lambda: sc.activation(va3, va3, AF.Exp, scale=consts[:, 1:2]))
            SC.wait(vc)
            b.V(lambda: vc.tensor_tensor(nm3, me3, va3, ALU.mult))
            for j in range(ntile3):
                x2 = y_acc[:, j * D:(j + 1) * D]
                OUTS[j % 2].wait(vc, x2_store.get(j % 2, 0))
                b.V(lambda j=j, x2=x2, slx=slx: vc.tensor_scalar(
                    x2[:, 0:D], x1c[slx][:, j * D:(j + 1) * D],
                    lnt38[:, mvo3 + 1 + 2 * j:mvo3 + 2 + 2 * j], nm3[:, j:j + 1],
                    ALU.mult, ALU.subtract))
                VC.wait(sy)
                rows = slice(t0 + j * 128, t0 + (j + 1) * 128)
                sy.dma_start(out_h[rows, :], x2[:, 0:D]).then_inc(OUTS[j % 2].h, 16)
                OUTS[j % 2].n += 16
                x2_store[j % 2] = OUTS[j % 2].n
            b._x1c_read[slx] = VC.n

        OUTS[0].wait(sy)
        OUTS[1].wait(sy)

    nc.finalize()
    return nc


def _head_perm(H=16, KVH=4, Dh=64):
    rep = H // KVH
    qperm = []
    for p in range(H // 2):
        e = 2 * (p // rep) * rep + p % rep
        qperm += [e, e + rep]
    rows = []
    for h in qperm:
        rows += list(range(Dh * h, Dh * h + Dh))
    return qperm, np.array(rows)


def make_inputs(cfg, kw, batch_idx):
    qperm, qrows = _head_perm()
    rowperm = np.concatenate([qrows, np.arange(1024, 1536)])
    d = {}
    d["x"] = np.ascontiguousarray(kw["L0_attn_hidden_states"][batch_idx])
    d["wqkv"] = np.ascontiguousarray(np.asarray(kw["L0_attn_qkv_w"])[rowperm].T)
    d["bqkv"] = np.ascontiguousarray(np.asarray(kw["L0_attn_qkv_b"])[rowperm])
    d["scaling"] = np.ascontiguousarray(kw["L0_attn_scaling"])
    d["wo"] = np.ascontiguousarray(np.asarray(kw["L0_attn_o_w"])[:, qrows].T)
    d["bo"] = np.ascontiguousarray(kw["L0_attn_o_b"])
    d["wgate"] = np.ascontiguousarray(kw["L0_ffn_gate_w"])
    d["bgate"] = np.ascontiguousarray(kw["L0_ffn_gate_bias"])
    d["wg"] = np.ascontiguousarray(kw["L0_ffn_w_gate"])
    d["wu"] = np.ascontiguousarray(kw["L0_ffn_w_up"])
    d["wd"] = np.ascontiguousarray(kw["L0_ffn_w_down"])
    d["swg"] = np.ascontiguousarray(kw["L0_ffn_sw_gate"])
    d["swu"] = np.ascontiguousarray(kw["L0_ffn_sw_up"])
    d["swd"] = np.ascontiguousarray(kw["L0_ffn_sw_down"])
    return {k: np.ascontiguousarray(v).astype(np.float32) for k, v in d.items()}


def kernel(**inputs):
    cfg = full_cfg()
    nc = build_program(cfg)
    in_maps = [make_inputs(cfg, inputs, i) for i in range(8)]
    res = run_bass_kernel_spmd(nc, in_maps, core_ids=list(range(8)))
    out = np.stack([res.results[i]["out"] for i in range(8)], axis=0)
    return out.astype(np.float32)


# version 39
# speedup vs baseline: 1.0263x; 1.0263x over previous
import math
import numpy as np
from contextlib import ExitStack

import concourse.bass as bass
import concourse.mybir as mybir
from concourse.bass_utils import run_bass_kernel_spmd

F32 = mybir.dt.float32
BF16 = mybir.dt.bfloat16
AF = mybir.ActivationFunctionType
ALU = mybir.AluOpType
AX = mybir.AxisListType

R_SOFTPLUS_0 = 1.442695041
NEG_C = -60.0
LN_EPS = 1e-5
ROUTED_SCALE = 2.5


def full_cfg():
    return dict(S=4096, D=1024, H=16, KVH=4, Dh=64, E=8, I=512, SHI=1024,
                QC=512, TCW=512, TC=1024, TS=256)


class Sem:
    def __init__(self, nc, name, dma=False):
        self.h = nc.alloc_semaphore(name)
        self.n = 0
        self.step = 16 if dma else 1

    def inc(self, instr, k=1):
        instr.then_inc(self.h, k * self.step)
        self.n += k * self.step
        return self.n

    def wait(self, eng, val=None):
        v = self.n if val is None else val
        if v > 0:
            eng.wait_ge(self.h, v)


class B:
    def __init__(self, nc, cfg):
        self.nc = nc
        c = dict(cfg)
        self.c = c
        c["kt_n"] = c["S"] // 128
        c["rep"] = c["H"] // c["KVH"]
        c["G"] = c["KVH"]
        c["di_n"] = c["D"] // 128
        c["nQ"] = c["H"] // 2
        c["nK"] = c["KVH"] // 2
        c["do_n"] = c["nQ"] + 2 * c["nK"]
        c["DO"] = c["do_n"] * 128
        c["n_qc"] = c["S"] // c["QC"]
        c["n_tcw"] = c["S"] // c["TCW"]
        c["ic_n"] = c["I"] // 128
        c["hc_n"] = c["D"] // 128
        c["n_tc"] = c["S"] // c["TC"]
        c["n_ts"] = c["TC"] // c["TS"]
        c["tsub_n"] = c["TC"] // 128
        c["n_sh"] = c["SHI"] // c["I"]
        c["nE"] = c["E"] + c["n_sh"]
        assert c["E"] == 8 and c["Dh"] == 64 and c["nQ"] == c["di_n"]
        self.TN = Sem(nc, "TN")
        self.VC = Sem(nc, "VC")
        self.SC = Sem(nc, "SC")
        self.GP = Sem(nc, "GP")
        self._epoch = {}

    def V(self, instr_fn):
        self.VC.wait(self.nc.vector)
        return self.VC.inc(instr_fn())

    def S(self, instr_fn):
        self.SC.wait(self.nc.scalar)
        return self.SC.inc(instr_fn())

    def G(self, instr_fn):
        self.GP.wait(self.nc.gpsimd)
        return self.GP.inc(instr_fn())

    def soft_barrier(self):
        nc = self.nc
        for eng in (nc.tensor, nc.vector, nc.scalar, nc.gpsimd, nc.sync):
            for s in (self.TN, self.VC, self.SC, self.GP):
                s.wait(eng)

    def epoch(self, name):
        self._epoch[name] = set()

    def mm_start(self, name, col_start, col_end):
        banks = set(range(col_start // 512, (col_end + 511) // 512))
        st = self._epoch[name]
        new = banks - st
        st |= banks
        return bool(new)


def _ap2(t, off, dims):
    a = t.ap()
    return bass.AP(a.tensor, off, [[a.ap[0][0], a.ap[0][1]]] + dims)


def _emit_ln(b, src, dst, sqs, lnt, consts, D):
    nc = b.nc
    vc, sc = nc.vector, nc.scalar
    CEPS, CNH = 0, 1
    b.V(lambda: vc.reduce_sum(lnt[:, 0:1], src[:, 0:D], axis=AX.X))
    b.VC.wait(sc)
    b.S(lambda: sc.activation(sqs[:, 0:D], src[:, 0:D], AF.Square,
                              accum_out=lnt[:, 1:2]))
    b.SC.wait(vc)
    b.V(lambda: vc.tensor_scalar_mul(lnt[:, 0:1], lnt[:, 0:1], 1.0 / D))
    b.V(lambda: vc.tensor_scalar_mul(lnt[:, 1:2], lnt[:, 1:2], 1.0 / D))
    b.V(lambda: vc.tensor_tensor(lnt[:, 4:5], lnt[:, 0:1], lnt[:, 0:1], ALU.mult))
    b.V(lambda: vc.tensor_sub(lnt[:, 2:3], lnt[:, 1:2], lnt[:, 4:5]))
    b.VC.wait(sc)
    b.S(lambda: sc.activation(lnt[:, 3:4], lnt[:, 2:3], AF.Ln,
                              bias=consts[:, CEPS:CEPS + 1]))
    b.S(lambda: sc.activation(lnt[:, 3:4], lnt[:, 3:4], AF.Exp,
                              scale=consts[:, CNH:CNH + 1]))
    b.SC.wait(vc)
    b.V(lambda: vc.tensor_tensor(lnt[:, 4:5], lnt[:, 0:1], lnt[:, 3:4], ALU.mult))
    b.V(lambda: vc.tensor_scalar(dst[:, 0:D], src[:, 0:D], lnt[:, 3:4],
                                 lnt[:, 4:5], ALU.mult, ALU.subtract))


def build_program(cfg):
    nc = bass.Bass("TRN2", target_bir_lowering=False)
    b = B(nc, cfg)
    c = b.c
    S, D, H, KVH, Dh = c["S"], c["D"], c["H"], c["KVH"], c["Dh"]
    E, I, SHI = c["E"], c["I"], c["SHI"]
    QC, TCW, TC, TS = c["QC"], c["TCW"], c["TC"], c["TS"]
    kt_n, rep, G = c["kt_n"], c["rep"], c["G"]
    di_n, nQ, nK, do_n, DO = c["di_n"], c["nQ"], c["nK"], c["do_n"], c["DO"]
    n_qc, n_tcw = c["n_qc"], c["n_tcw"]
    ic_n, hc_n, n_tc, n_ts, tsub_n, nE = (
        c["ic_n"], c["hc_n"], c["n_tc"], c["n_ts"], c["tsub_n"], c["nE"])

    x_h = nc.dram_tensor("x", [S, D], F32, kind="ExternalInput").ap()
    wqkv_h = nc.dram_tensor("wqkv", [D, DO], F32, kind="ExternalInput").ap()
    bqkv_h = nc.dram_tensor("bqkv", [DO], F32, kind="ExternalInput").ap()
    scal_h = nc.dram_tensor("scaling", [Dh], F32, kind="ExternalInput").ap()
    wo_h = nc.dram_tensor("wo", [D, D], F32, kind="ExternalInput").ap()
    bo_h = nc.dram_tensor("bo", [D], F32, kind="ExternalInput").ap()
    wgate_h = nc.dram_tensor("wgate", [D, E], F32, kind="ExternalInput").ap()
    bgate_h = nc.dram_tensor("bgate", [E], F32, kind="ExternalInput").ap()
    wg_h = nc.dram_tensor("wg", [E, D, I], F32, kind="ExternalInput").ap()
    wu_h = nc.dram_tensor("wu", [E, D, I], F32, kind="ExternalInput").ap()
    wd_h = nc.dram_tensor("wd", [E, I, D], F32, kind="ExternalInput").ap()
    swg_h = nc.dram_tensor("swg", [D, SHI], F32, kind="ExternalInput").ap()
    swu_h = nc.dram_tensor("swu", [D, SHI], F32, kind="ExternalInput").ap()
    swd_h = nc.dram_tensor("swd", [SHI, D], F32, kind="ExternalInput").ap()
    out_h = nc.dram_tensor("out", [S, D], F32, kind="ExternalOutput").ap()
    x1s_h = nc.dram_tensor("x1s", [S, D], F32, kind="Internal").ap()
    x1b_h = nc.dram_tensor("x1b", [S, D], BF16, kind="Internal").ap()
    EWCOL0 = max((D // 128) * I, (I // 128) * D)
    nE0 = E + SHI // I
    ewb_h = nc.dram_tensor("ewb", [nE0, 3, 128, EWCOL0], BF16, kind="Internal").ap()

    ident = nc.alloc_sbuf_tensor("ident", [128, 128], F32)
    consts = nc.alloc_sbuf_tensor("consts", [128, 8], F32)
    CEPS, CNH, C1, CN1, CNC = 0, 1, 2, 3, 4
    bgate8 = nc.alloc_sbuf_tensor("bgate8", [E, 1], F32)
    ones1 = nc.alloc_sbuf_tensor("ones1", [1, 128], F32)
    ones1b = nc.alloc_sbuf_tensor("ones1b", [1, 128], BF16)
    attn_scope = ExitStack()
    _a = attn_scope.enter_context
    scal64 = _a(nc.sbuf_tensor("scal64", [64, 4], F32))
    scale128 = _a(nc.sbuf_tensor("scale128", [128, 1], F32))
    bq = _a(nc.sbuf_tensor("bq", [128, do_n], F32))
    beff = _a(nc.sbuf_tensor("beff", [128, do_n], F32))
    obB = _a(nc.sbuf_tensor("obB", [128, D], F32))
    QT = [_a(nc.sbuf_tensor(f"QT{p}", [128, S], BF16)) for p in range(nQ)]
    KTC = [_a(nc.sbuf_tensor(f"KTC{k}", [128, S], BF16)) for k in range(nK)]
    Vp = [_a(nc.sbuf_tensor(f"Vp{g}", [128, 128 * kt_n], BF16)) for g in range(G)]
    p0_scope = ExitStack()
    bo_row = p0_scope.enter_context(nc.sbuf_tensor("bo_row", [1, D], F32))

    psS = nc.alloc_psum_tensor("psS", [128, 2048], F32)
    psC = nc.alloc_psum_tensor("psC", [128, 2048], F32)

    class _PsView:

        def __init__(self, off):
            self.off = off

        def __getitem__(self, idx):
            ps_, fs = idx
            st = self.off + (fs.start or 0)
            sp = self.off + (fs.stop if fs.stop is not None else 1024)
            return psS[ps_, st:sp]

    psA = _PsView(0)
    psB = _PsView(1024)

    sy, tn, vc, sc, gp = nc.sync, nc.tensor, nc.vector, nc.scalar, nc.gpsimd
    TN, VC, SC, GP = b.TN, b.VC, b.SC, b.GP

    LD = Sem(nc, "LD", dma=True)
    WS = [Sem(nc, f"WS{i}", dma=True) for i in range(2)]
    X0 = [Sem(nc, f"X0{i}", dma=True) for i in range(4)]
    X0R = [Sem(nc, f"X0R{i}", dma=True) for i in range(2)]
    X1W = [Sem(nc, f"X1W{i}", dma=True) for i in range(2)]
    X1R = Sem(nc, "X1R", dma=True)
    X1B = [Sem(nc, f"X1B{i}", dma=True) for i in range(2)]
    XF = Sem(nc, "XF", dma=True)
    X0T = [Sem(nc, f"X0T{i}", dma=True) for i in range(4)]
    X1R2 = Sem(nc, "X1R2", dma=True)
    EW = [Sem(nc, f"EW{i}", dma=True) for i in range(2)]
    WR = Sem(nc, "WR", dma=True)
    EWI = [Sem(nc, f"EWI{i}", dma=True) for i in range(2)]
    EWO = Sem(nc, "EWO", dma=True)
    OUTS = [Sem(nc, f"OUTS{i}", dma=True) for i in range(2)]

    for k, v in {CEPS: LN_EPS, CNH: -0.5, C1: 1.0, CN1: -1.0, CNC: NEG_C}.items():
        b.V(lambda k=k, v=v: vc.memset(consts[:, k:k + 1], v))
    b.V(lambda: vc.memset(ident[:], 0.0))
    b.V(lambda: vc.memset(ones1[0:1, :], 1.0))
    b.V(lambda: vc.memset(ones1b[0:1, :], 1.0))
    VC.wait(gp)
    b.G(lambda: gp.affine_select(out=ident[:], in_=ident[:],
                                 compare_op=ALU.not_equal, fill=1.0, base=0,
                                 pattern=[[-1, 128]], channel_multiplier=1))
    for g in range(G):
        b.V(lambda g=g: vc.memset(Vp[g][:], 1.0))

    sy.dma_start(scal64[:, 0:1], scal_h.rearrange("(d a) -> d a", a=1)).then_inc(LD.h, 16)
    LD.n += 16
    for _c in range(do_n):
        sy.dma_start(bq[:, _c:_c + 1],
                     bqkv_h[_c * 128:(_c + 1) * 128].rearrange("(p a) -> p a", a=1),
                     allow_slow_non_contiguous=True).then_inc(LD.h, 16)
        LD.n += 16
    sy.dma_start(bo_row[0:1, :], bo_h.rearrange("(a d) -> a d", a=1)).then_inc(LD.h, 16)
    LD.n += 16
    sy.dma_start(bgate8[:, 0:1], bgate_h.rearrange("(e a) -> e a", a=1)).then_inc(LD.h, 16)
    LD.n += 16
    ld_p0 = LD.n

    LD.wait(vc, ld_p0)
    b.V(lambda: vc.tensor_scalar_mul(scal64[:, 1:2], scal64[:, 0:1], -1.0))
    b.V(lambda: vc.tensor_max(scal64[:, 1:2], scal64[:, 0:1], scal64[:, 1:2]))
    b.V(lambda: vc.tensor_single_scalar(scal64[:, 2:3], scal64[:, 0:1], 0.0, ALU.max))
    VC.wait(sc)
    b.S(lambda: sc.activation(scal64[:, 1:2], scal64[:, 1:2], AF.Exp,
                              scale=consts[0:64, CN1:CN1 + 1]))
    b.S(lambda: sc.activation(scal64[:, 1:2], scal64[:, 1:2], AF.Ln,
                              bias=consts[0:64, C1:C1 + 1]))
    SC.wait(vc)
    b.V(lambda: vc.tensor_add(scal64[:, 3:4], scal64[:, 2:3], scal64[:, 1:2]))
    b.V(lambda: vc.tensor_scalar_mul(scal64[:, 3:4], scal64[:, 3:4],
                                     R_SOFTPLUS_0 / math.sqrt(Dh)))
    b.V(lambda: vc.tensor_copy(scale128[0:64, 0:1], scal64[:, 3:4]))
    b.V(lambda: vc.tensor_copy(scale128[64:128, 0:1], scal64[:, 3:4]))
    LD.wait(vc, ld_p0)
    for dd in range(do_n):
        if dd < nQ:
            b.V(lambda dd=dd: vc.tensor_tensor(beff[:, dd:dd + 1], bq[:, dd:dd + 1],
                                               scale128[:, 0:1], ALU.mult))
        else:
            b.V(lambda dd=dd: vc.tensor_copy(beff[:, dd:dd + 1], bq[:, dd:dd + 1]))
    LD.wait(tn, ld_p0)
    VC.wait(tn)
    for m in range((D + 511) // 512):
        w = min(512, D - m * 512)
        TN.inc(tn.matmul(psC[0:128, m * 512:m * 512 + w], ones1[0:1, :],
                         bo_row[0:1, m * 512:m * 512 + w], start=True, stop=True))
    TN.wait(vc)
    b.V(lambda: vc.tensor_copy(obB[:, :], psC[:, 0:D]))

    nc.all_engine_barrier()
    p0_scope.close()

    ntiles = TCW // 128
    with ExitStack() as p1:
        wqkvb = p1.enter_context(nc.sbuf_tensor("wqkvb", [128, di_n * DO], BF16))
        stage = [p1.enter_context(nc.sbuf_tensor("wst0", [128, DO], F32))]
        x0b = [p1.enter_context(nc.sbuf_tensor(f"x0b{i}", [128, D], F32))
               for i in range(4)]
        x0c = [p1.enter_context(nc.sbuf_tensor(f"x0c{i}", [128, D], BF16))
               for i in range(4)]
        x0f = [p1.enter_context(nc.sbuf_tensor(f"x0f{i}", [128, di_n * TCW], BF16))
               for i in range(2)]
        vtmp = [p1.enter_context(nc.sbuf_tensor(f"vtmp{i}", [64, TCW], F32))
                for i in range(2)]

        for di in range(di_n):
            VC.wait(sy)
            sy.dma_start(stage[0][:], wqkv_h[di * 128:(di + 1) * 128, :]).then_inc(WS[0].h, 16)
            WS[0].n += 16
            WS[0].wait(vc)
            b.V(lambda di=di: vc.tensor_copy(wqkvb[:, di * DO:(di + 1) * DO],
                                             stage[0][:]))
        wqkv_ready = VC.n

        pslot_free = {}
        qkv_slot_free = {}
        vtmp_free = 0
        b._x0cast = {}
        b._x0t = {}
        b._qkv_last = {}
        for tc in range(n_tcw):
            for j in range(ntiles):
                s = (tc * ntiles + j) % 4
                VC.wait(sy, b._x0cast.get(s, 0))
                t0r = tc * TCW + j * 128
                sy.dma_start(x0b[s][:], x_h[t0r:t0r + 128, :]).then_inc(X0[s].h, 16)
                X0[s].n += 16
                X0[s].wait(vc)
                X0T[s].wait(vc, b._x0t.get(s, 0))
                b.V(lambda s=s: vc.tensor_copy(x0c[s][:, :], x0b[s][:, 0:D]))
                b._x0cast[s] = VC.n
                VC.wait(sy, b._x0cast[s])
                xsl = tc % 2
                TN.wait(sy, b._qkv_last.get(xsl, 0))
                X0T[s].wait(sy, b._x0t.get(s, 0))
                sy.dma_start_transpose(
                    bass.AP(x0f[xsl].ap().tensor, j * 128,
                            [[x0f[xsl].ap().ap[0][0], 128], [TCW, di_n], [1, 128]]),
                    x0c[s][:, :]).then_inc(X0T[s].h, 16)
                X0T[s].n += 16
                b._x0t[s] = X0T[s].n
            x0f_dma = tuple(X0T[i].n for i in range(4))
            xsl = tc % 2

            for dd in range(do_n):
                pst, off = [(psA, 0), (psA, 512), (psB, 0), (psB, 512)][dd % 4]
                pname = (id(pst), off)
                SC.wait(tn, qkv_slot_free.get(pname, 0))
                VC.wait(tn, wqkv_ready)
                for _i in range(4):
                    X0T[_i].wait(tn, x0f_dma[_i])
                for di in range(di_n):
                    ins = tn.matmul(pst[:, off:off + TCW],
                                    wqkvb[:, di * DO + dd * 128: di * DO + (dd + 1) * 128],
                                    x0f[xsl][:, di * TCW:(di + 1) * TCW],
                                    start=(di == 0), stop=(di == di_n - 1))
                TN.inc(ins)
                mm_done = TN.n
                TN.wait(sc, mm_done)
                if dd < nQ:
                    b.S(lambda dd=dd, tc=tc, pst=pst, off=off: sc.activation(
                        QT[dd][:, tc * TCW:(tc + 1) * TCW], pst[:, off:off + TCW],
                        AF.Identity, scale=scale128[:, 0:1], bias=beff[:, dd:dd + 1]))
                    qkv_slot_free[pname] = SC.n
                elif dd < nQ + nK:
                    kc = dd - nQ
                    cols = slice(tc * TCW, (tc + 1) * TCW)
                    b.S(lambda dd=dd, pst=pst, off=off, kc=kc, cols=cols:
                        sc.activation(KTC[kc][:, cols], pst[:, off:off + TCW],
                                      AF.Identity, bias=beff[:, dd:dd + 1]))
                    qkv_slot_free[pname] = SC.n
                else:
                    vcc = dd - nQ - nK
                    TN.wait(sc, vtmp_free)
                    b.S(lambda dd=dd, pst=pst, off=off: sc.activation(
                        vtmp[0][0:64, :], pst[0:64, off:off + TCW], AF.Identity,
                        bias=beff[0:64, dd:dd + 1]))
                    b.S(lambda dd=dd, pst=pst, off=off: sc.activation(
                        vtmp[1][0:64, :], pst[64:128, off:off + TCW], AF.Identity,
                        bias=beff[64:128, dd:dd + 1]))
                    qkv_slot_free[pname] = SC.n
                    vevac = SC.n
                    for ktl in range(ntiles):
                        for gl in range(2):
                            g = 2 * vcc + gl
                            kt = tc * ntiles + ktl
                            sl = (ktl * 2 + gl) % 4
                            co = sl * 512
                            SC.wait(tn, vevac)
                            VC.wait(tn, pslot_free.get(sl, 0))
                            TN.inc(tn.transpose(psC[:, co:co + 64],
                                                vtmp[gl][0:64, ktl * 128:(ktl + 1) * 128],
                                                ident[0:64, 0:64]))
                            TN.wait(vc)
                            vdo = 128 * kt + (64 if g % 2 else 0)
                            b.V(lambda g=g, vdo=vdo, co=co: vc.tensor_copy(
                                Vp[g][:, vdo:vdo + 64],
                                psC[:, co:co + 64]))
                            pslot_free[sl] = VC.n
                    vtmp_free = TN.n
            b._qkv_last[xsl] = TN.n

    X0[0].wait(sy)
    X0[1].wait(sy)
    WS[0].wait(sy)
    nc.all_engine_barrier()

    with ExitStack() as p2:
        attnF = [p2.enter_context(nc.sbuf_tensor(f"attnF{p}", [128, QC], BF16))
                 for p in range(nQ)]
        woB = p2.enter_context(nc.sbuf_tensor("woB", [128, di_n * D], BF16))
        pb = [p2.enter_context(nc.sbuf_tensor(f"pb{i}", [128, 2 * QC], BF16))
              for i in range(4)]
        mrgP = p2.enter_context(nc.sbuf_tensor("mrgP", [128, QC], F32))
        mrgS = p2.enter_context(nc.sbuf_tensor("mrgS", [128, QC], F32))
        mrgR = p2.enter_context(nc.sbuf_tensor("mrgR", [128, QC], F32))
        CW0 = min(1024, max(hc_n * I, ic_n * D))
        cstage = [p2.enter_context(nc.sbuf_tensor("cstage0", [128, CW0], F32))]
        x1bb = [p2.enter_context(nc.sbuf_tensor(f"x1bb{i}", [128, D], BF16))
                for i in range(2)]
        x0r = [p2.enter_context(nc.sbuf_tensor(f"x0r{i}", [128, D], F32))
               for i in range(2)]
        r1_all = p2.enter_context(nc.sbuf_tensor("r1_all", [128, (QC // 128) * D], F32))
        x1t = [p2.enter_context(nc.sbuf_tensor(f"x1t{i}", [128, D], F32))
               for i in range(2)]
        lnt8 = p2.enter_context(nc.sbuf_tensor("lnt8", [128, 64], F32))
        ewcast = p2.enter_context(nc.sbuf_tensor("ewcast", [128, CW0], BF16))

        for di in range(di_n):
            VC.wait(sy)
            sy.dma_start(cstage[0][:, 0:D],
                         wo_h[di * 128:(di + 1) * 128, :]).then_inc(WS[0].h, 16)
            WS[0].n += 16
            WS[0].wait(vc)
            b.V(lambda di=di: vc.tensor_copy(woB[:, di * D:(di + 1) * D],
                                             cstage[0][:, 0:D]))
        wo_ready = VC.n
        VC_after_wo = VC.n

        EWCOL = max(hc_n * I, ic_n * D)
        CW = min(1024, EWCOL)
        cast_jobs = []
        for e in range(nE):
            if e < E:
                srcs = [wg_h[e], wu_h[e], wd_h[e]]
            else:
                hh = e - E
                srcs = [swg_h[:, hh * I:(hh + 1) * I],
                        swu_h[:, hh * I:(hh + 1) * I],
                        swd_h[hh * I:(hh + 1) * I, :]]
            for m in range(3):
                ncols = I if m < 2 else D
                cpc = CW // ncols
                for k in range(EWCOL // CW):
                    cast_jobs.append((e, m, k, srcs[m], ncols, cpc))
        b._cast_issue = 0
        b._cast_done = 0
        b._cast_gp = {}

        def issue_cast_dma():
            if b._cast_issue >= len(cast_jobs):
                return
            i = b._cast_issue
            b._cast_issue += 1
            e, m, k, src, ncols, cpc = cast_jobs[i]
            sl = 0
            if i == 0:
                VC.wait(gp, VC_after_wo)
            gp.dma_start(
                cstage[sl].ap().rearrange("p (c i) -> p c i", c=cpc)[:, :, 0:ncols],
                src.rearrange("(c p) i -> p c i", p=128)[:, k * cpc:(k + 1) * cpc, :],
            ).then_inc(EWI[sl].h, 16)
            EWI[sl].n += 16

        def emit_cast_chunk():
            if b._cast_done >= b._cast_issue:
                return
            i = b._cast_done
            b._cast_done += 1
            e, m, k, src, ncols, cpc = cast_jobs[i]
            sl = 0
            EWI[sl].wait(gp, (i + 1) * 16)
            EWO.wait(gp)
            b.G(lambda sl=sl, n=cpc * ncols: gp.tensor_copy(
                ewcast[:, 0:n], cstage[sl][:, 0:n]))
            b._cast_gp[sl] = GP.n
            GP.wait(gp, b._cast_gp[sl])
            gp.dma_start(ewb_h[e, m, :, k * CW:(k + 1) * CW],
                         ewcast[:, 0:CW]).then_inc(EWO.h, 16)
            EWO.n += 16
        issue_cast_dma()

        b._expslot = {}
        b._pbslot = {}
        b._psC_free_v = 0
        b._oproj_tn = 0
        b._attn_div_v = 0
        b._x1st = {}
        b._x1bt = {}
        b._fin_v = {}
        oslot_free = {}
        pending_fin = None

        def emit_fin(st):
            p, pvlast, first_of_qc = st
            TN.wait(vc, pvlast)
            if first_of_qc:
                TN.wait(vc, b._oproj_tn)
            b.V(lambda: vc.tensor_copy(mrgP[0:64, :], psC[0:64, 0:QC]))
            b.V(lambda: vc.tensor_add(mrgP[0:64, :], mrgP[0:64, :],
                                      psC[0:64, 512:512 + QC]))
            b.V(lambda: vc.tensor_copy(mrgS[64:128, :], psC[64:128, 0:QC]))
            b.V(lambda: vc.tensor_add(mrgS[64:128, :], mrgS[64:128, :],
                                      psC[64:128, 512:512 + QC]))
            b.V(lambda: vc.tensor_copy(mrgP[64:128, :], psC[64:128, 1024:1024 + QC]))
            b.V(lambda: vc.tensor_add(mrgP[64:128, :], mrgP[64:128, :],
                                      psC[64:128, 1536:1536 + QC]))
            b.V(lambda: vc.tensor_copy(mrgS[0:64, :], psC[0:64, 1024:1024 + QC]))
            b.V(lambda: vc.tensor_add(mrgS[0:64, :], mrgS[0:64, :],
                                      psC[0:64, 1536:1536 + QC]))
            b._psC_free_v = VC.n
            b.V(lambda: vc.reciprocal(mrgR[0:64, :], mrgS[64:128, :]))
            b.V(lambda: vc.reciprocal(mrgR[64:128, :], mrgS[0:64, :]))
            b.V(lambda p=p: vc.tensor_tensor(
                attnF[p][0:64, :], mrgP[0:64, :], mrgR[0:64, :], ALU.mult))
            b.V(lambda p=p: vc.tensor_tensor(
                attnF[p][64:128, :], mrgP[64:128, :], mrgR[64:128, :], ALU.mult))
            b._attn_div_v = VC.n
            b._fin_v[p] = VC.n

        pending_tail = None
        ntile = QC // 128
        nbs = (D + 511) // 512
        mvo = 6 * nbs * ntile

        def emit_ln_tail(q0t):
            for j in range(ntile):
                b.V(lambda j=j: vc.bn_aggr(
                    lnt8[:, mvo + 2 * j:mvo + 2 * j + 2],
                    lnt8[:, 6 * nbs * j:6 * nbs * (j + 1)]))
            me = _ap2(lnt8, mvo, [[2, ntile]])
            va = _ap2(lnt8, mvo + 1, [[2, ntile]])
            nm = lnt8[:, mvo + 2 * ntile:mvo + 3 * ntile]
            VC.wait(sc)
            b.S(lambda: sc.activation(va, va, AF.Ln, bias=consts[:, 0:1]))
            b.S(lambda: sc.activation(va, va, AF.Exp, scale=consts[:, 1:2]))
            SC.wait(vc)
            b.V(lambda: vc.tensor_tensor(nm, me, va, ALU.mult))
            for j in range(ntile):
                x1 = x1t[j % 2]
                X1W[j % 2].wait(vc, b._x1st.get(j % 2, 0))
                b.V(lambda j=j, x1=x1: vc.tensor_scalar(
                    x1[:, 0:D], r1_all[:, j * D:(j + 1) * D],
                    lnt8[:, mvo + 1 + 2 * j:mvo + 2 + 2 * j], nm[:, j:j + 1],
                    ALU.mult, ALU.subtract))
                X1B[j % 2].wait(vc, b._x1bt.get(j % 2, 0))
                b.V(lambda j=j, x1=x1: vc.tensor_copy(x1bb[j % 2][:, :],
                                                      x1[:, 0:D]))
                VC.wait(sy)
                rows = slice(q0t + j * 128, q0t + (j + 1) * 128)
                sy.dma_start(x1s_h[rows, :], x1[:]).then_inc(X1W[j % 2].h, 16)
                X1W[j % 2].n += 16
                sy.dma_start(x1b_h[rows, :], x1bb[j % 2][:, :]).then_inc(X1B[j % 2].h, 16)
                X1B[j % 2].n += 16
                b._x1st[j % 2] = X1W[j % 2].n
                b._x1bt[j % 2] = X1B[j % 2].n

        for qc in range(n_qc):
            q0 = qc * QC
            NB = nQ * kt_n
            exp_val = {}
            scores_done = {}
            for v in range(NB + 4):
                if v >= 4:
                    u = v - 4
                    pP, ktP = divmod(u, kt_n)
                    if ktP == 0 and pending_fin is not None:
                        emit_fin(pending_fin)
                        pending_fin = None
                    kcP = pP // rep
                    gE, gO = 2 * kcP, 2 * kcP + 1
                    slp = u % 4
                    vo = ktP * 128
                    st = (ktP == 0)
                    sp = (ktP == kt_n - 1)
                    SC.wait(tn, exp_val[u])
                    if st:
                        VC.wait(tn, b._psC_free_v)
                    tn.matmul(psC[0:128, 0:QC], Vp[gE][0:64, vo:vo + 128],
                              pb[slp][0:64, 0:QC], start=st, stop=sp,
                              tile_position=(0, 0))
                    tn.matmul(psC[0:128, 512:512 + QC],
                              Vp[gE][64:128, vo:vo + 128],
                              pb[slp][64:128, 0:QC], start=st, stop=sp,
                              tile_position=(64, 0))
                    tn.matmul(psC[0:128, 1024:1024 + QC],
                              Vp[gO][0:64, vo:vo + 128],
                              pb[slp][0:64, QC:2 * QC], start=st, stop=sp,
                              tile_position=(0, 0))
                    TN.inc(tn.matmul(psC[0:128, 1536:1536 + QC],
                                     Vp[gO][64:128, vo:vo + 128],
                                     pb[slp][64:128, QC:2 * QC], start=st,
                                     stop=sp, tile_position=(64, 0)))
                    b._pbslot[slp] = TN.n
                    if sp:
                        pending_fin = (pP, TN.n, pP == 0)
                if v < NB:
                    pS, ktS = divmod(v, kt_n)
                    sl = v % 2
                    so = sl * 1024
                    ko = ktS * 128
                    kcS = pS // rep
                    SC.wait(tn, b._expslot.get(sl, 0))
                    tn.matmul(psS[0:128, so:so + QC],
                              KTC[kcS][0:64, ko:ko + 128],
                              QT[pS][0:64, q0:q0 + QC],
                              start=True, stop=True, tile_position=(0, 0))
                    TN.inc(tn.matmul(psS[0:128, so + QC:so + 2 * QC],
                                     KTC[kcS][64:128, ko:ko + 128],
                                     QT[pS][64:128, q0:q0 + QC],
                                     start=True, stop=True,
                                     tile_position=(64, 0)))
                    scores_done[v] = TN.n
                if 1 <= v <= NB:
                    ke = v - 1
                    sl = ke % 2
                    TN.wait(sc, scores_done[ke])
                    TN.wait(sc, b._pbslot.get(ke % 4, 0))
                    SC.inc(sc.activation(pb[ke % 4][:, 0:2 * QC],
                                         psS[:, sl * 1024:sl * 1024 + 2 * QC],
                                         AF.Exp, bias=consts[:, CNC:CNC + 1]))
                    exp_val[ke] = SC.n
                    b._expslot[sl] = SC.n
                if v == 6 and pending_tail is not None:
                    emit_ln_tail(pending_tail)
                    pending_tail = None
                if v % 8 == 4:
                    emit_cast_chunk()
                    issue_cast_dma()
            if pending_fin is not None:
                emit_fin(pending_fin)
                pending_fin = None

            oNC = (D + 511) // 512
            for j in range(ntile):
                sq = j % 2
                po = 1024 * sq
                rows = slice(q0 + j * 128, q0 + (j + 1) * 128)
                VC.wait(sy, oslot_free.get(("x", sq), 0))
                sy.dma_start(x0r[sq][:, 0:D], x_h[rows, :]).then_inc(X0R[sq].h, 16)
                X0R[sq].n += 16
                VC.wait(tn, max(wo_ready, b._psC_free_v,
                                oslot_free.get(("ps", sq), 0)))
                for m in range(oNC):
                    w = min(512, D - m * 512)
                    for di in range(di_n):
                        VC.wait(tn, b._fin_v.get(di, 0))
                        ins = tn.matmul(psC[0:128, po + m * 512:po + m * 512 + w],
                                        attnF[di][:, j * 128:(j + 1) * 128],
                                        woB[:, di * D + m * 512: di * D + m * 512 + w],
                                        start=(di == 0), stop=(di == di_n - 1))
                TN.inc(ins)
                odone = TN.n
                b._oproj_tn = TN.n
                TN.wait(vc, odone)
                X0R[sq].wait(vc)
                dst = r1_all[:, j * D:(j + 1) * D]
                b.V(lambda po=po, sq=sq, dst=dst: vc.tensor_add(
                    dst, psC[0:128, po:po + D], x0r[sq][:, 0:D]))
                b.V(lambda dst=dst: vc.tensor_add(dst, dst, obB[:, :]))
                oslot_free[("x", sq)] = VC.n
                oslot_free[("ps", sq)] = VC.n
                nbs = (D + 511) // 512
                for c in range(nbs):
                    w = min(512, D - c * 512)
                    b.V(lambda j=j, c=c, w=w, dst=dst, nbs=nbs: vc.bn_stats(
                        lnt8[:, 6 * (nbs * j + c):6 * (nbs * j + c) + 6],
                        dst[:, c * 512:c * 512 + w]))
            b._psC_free_v = VC.n
            pending_tail = q0

        if pending_tail is not None:
            emit_ln_tail(pending_tail)
            pending_tail = None
        while b._cast_done < len(cast_jobs):
            emit_cast_chunk()
            issue_cast_dma()
        X0R[0].wait(sy)
        X0R[1].wait(sy)
        X1B[0].wait(sy)
        X1B[1].wait(sy)
        WS[0].wait(sy)
        EWI[0].wait(sy)
        EWI[1].wait(sy)
        EWO.wait(sy)

    nc.all_engine_barrier()
    attn_scope.close()

    with ExitStack() as p3:
        x1c = [p3.enter_context(nc.sbuf_tensor(f"x1c{i}", [128, tsub_n * D], F32))
               for i in range(2)]
        x1f = p3.enter_context(nc.sbuf_tensor("x1f", [128, di_n * TC], BF16))
        wb = p3.enter_context(nc.sbuf_tensor("wb", [128, E * TC], BF16))
        y_acc = p3.enter_context(nc.sbuf_tensor("y_acc", [128, hc_n * TC], F32))
        gs = [p3.enter_context(nc.sbuf_tensor(f"gs{i}", [128, ic_n * TS], BF16))
              for i in range(2)]
        guf = p3.enter_context(nc.sbuf_tensor("guf", [128, ic_n * TS], BF16))
        gu = [p3.enter_context(nc.sbuf_tensor(f"gu{i}", [128, ic_n * TS], BF16))
              for i in range(2)]
        wgateB = p3.enter_context(nc.sbuf_tensor("wgateB", [128, di_n * E], BF16))
        wgstage = p3.enter_context(nc.sbuf_tensor("wgstage", [128, di_n * E], F32))
        nsub = S // 128
        wFrow = p3.enter_context(nc.sbuf_tensor("wFrow", [E, S], BF16))
        wrowB1 = p3.enter_context(nc.sbuf_tensor("wrowB1", [1, TC], BF16))
        lnt38 = p3.enter_context(nc.sbuf_tensor("lnt38", [128, 128], F32))
        pg = ExitStack()
        _g = pg.enter_context
        sfc_f = _g(nc.sbuf_tensor("sfc_f", [E, TC], F32))
        st_t = _g(nc.sbuf_tensor("st_t", [128, nsub * E], F32))
        gsum = _g(nc.sbuf_tensor("gsum", [128, nsub * 4], F32))
        cntg = _g(nc.sbuf_tensor("cntg", [128, nsub * 4], F32))
        selg = _g(nc.sbuf_tensor("selg", [128, nsub * 4], F32))
        tmpe = _g(nc.sbuf_tensor("tmpe", [128, nsub * E], F32))
        cnte = _g(nc.sbuf_tensor("cnte", [128, nsub * E], F32))
        wraw = _g(nc.sbuf_tensor("wraw", [128, nsub * E], F32))
        scr8 = _g(nc.sbuf_tensor("scr8", [128, nsub * E], F32))
        den = _g(nc.sbuf_tensor("den", [128, nsub], F32))

        sy.dma_start(wgstage.ap().rearrange("p (c e) -> p c e", c=di_n),
                     wgate_h.rearrange("(c p) e -> p c e", p=128)).then_inc(LD.h, 16)
        LD.n += 16
        LD.wait(vc)
        b.V(lambda: vc.tensor_copy(wgateB[:, :], wgstage[:, :]))
        wgate_ready = VC.n

        pslot_free = {}
        b._gate_sc = 0
        for tcg in list(range(1, n_tc)) + [0]:
            tg0 = tcg * TC
            X1B[0].wait(sy)
            X1B[1].wait(sy)
            XF.wait(sy)
            TN.wait(sy)
            sy.dma_start_transpose(
                bass.AP(x1f.ap().tensor, 0,
                        [[x1f.ap().ap[0][0], 128], [TC, di_n], [1, TC]]),
                x1b_h[tg0:tg0 + TC, :]).then_inc(XF.h, 16)
            XF.n += 16
            b._xf_val = XF.n
            XF.wait(tn, b._xf_val)
            VC.wait(tn, wgate_ready)
            SC.wait(tn)
            for di in range(di_n):
                for m in range((TC + 511) // 512):
                    w = min(512, TC - m * 512)
                    ins = tn.matmul(psA[0:E, m * 512:m * 512 + w],
                                    wgateB[:, di * E:(di + 1) * E],
                                    x1f[:, di * TC + m * 512: di * TC + m * 512 + w],
                                    start=(di == 0), stop=(di == di_n - 1))
            TN.inc(ins)
            TN.wait(sc)
            b.S(lambda: sc.activation(sfc_f[:, :], psA[0:E, 0:TC], AF.Sigmoid))
            b._gate_sc = SC.n
            SC.wait(vc)
            LD.wait(vc)
            b.V(lambda: vc.tensor_scalar_add(sfc_f[:, :], sfc_f[:, :], bgate8[:, 0:1]))
            sfc_ready = VC.n
            for j in range(tsub_n):
                jj = tcg * tsub_n + j
                sl = jj % 4
                co = sl * 512
                VC.wait(tn, max(pslot_free.get(sl, 0), sfc_ready))
                TN.inc(tn.transpose(psC[0:128, co:co + E],
                                    sfc_f[:, j * 128:(j + 1) * 128],
                                    ident[0:E, 0:E]))
                TN.wait(vc)
                b.V(lambda jj=jj, co=co: vc.tensor_copy(
                    st_t[:, jj * E:(jj + 1) * E], psC[:, co:co + E]))
                pslot_free[sl] = VC.n

        def sl_e(tens, e, n=E):
            return _ap2(tens, e, [[n, nsub]])

        for g4 in range(4):
            b.V(lambda g4=g4: vc.tensor_tensor(sl_e(gsum, g4, 4),
                                               sl_e(st_t, 2 * g4),
                                               sl_e(st_t, 2 * g4 + 1), ALU.add))
        for g4 in range(4):
            others = [x for x in range(4) if x != g4]
            b.V(lambda g4=g4, o=others[0]: vc.tensor_tensor(
                sl_e(cntg, g4, 4), sl_e(gsum, o, 4), sl_e(gsum, g4, 4), ALU.is_gt))
            for o in others[1:]:
                b.V(lambda g4=g4, o=o: vc.tensor_tensor(
                    sl_e(selg, g4, 4), sl_e(gsum, o, 4), sl_e(gsum, g4, 4), ALU.is_gt))
                b.V(lambda g4=g4: vc.tensor_tensor(
                    sl_e(cntg, g4, 4), sl_e(cntg, g4, 4), sl_e(selg, g4, 4), ALU.add))
        for g4 in range(4):
            b.V(lambda g4=g4: vc.tensor_single_scalar(
                sl_e(selg, g4, 4), sl_e(cntg, g4, 4), 2.0, ALU.is_lt))
        for e in range(E):
            b.V(lambda e=e: vc.tensor_tensor(
                sl_e(tmpe, e), sl_e(st_t, e), sl_e(selg, e // 2, 4), ALU.mult))
        for e in range(E):
            others = [x for x in range(E) if x != e]
            b.V(lambda e=e, o=others[0]: vc.tensor_tensor(
                sl_e(cnte, e), sl_e(tmpe, o), sl_e(tmpe, e), ALU.is_gt))
            for o in others[1:]:
                b.V(lambda e=e, o=o: vc.tensor_tensor(
                    sl_e(scr8, e), sl_e(tmpe, o), sl_e(tmpe, e), ALU.is_gt))
                b.V(lambda e=e: vc.tensor_tensor(
                    sl_e(cnte, e), sl_e(cnte, e), sl_e(scr8, e), ALU.add))
        for e in range(E):
            b.V(lambda e=e: vc.tensor_single_scalar(
                sl_e(scr8, e), sl_e(cnte, e), 2.0, ALU.is_lt))
            b.V(lambda e=e: vc.tensor_single_scalar(
                sl_e(cnte, e), sl_e(tmpe, e), 0.0, ALU.is_gt))
            b.V(lambda e=e: vc.tensor_tensor(
                sl_e(scr8, e), sl_e(scr8, e), sl_e(cnte, e), ALU.mult))
            b.V(lambda e=e: vc.tensor_tensor(
                sl_e(wraw, e), sl_e(scr8, e), sl_e(st_t, e), ALU.mult))
        den_ap = _ap2(den, 0, [[1, nsub]])
        b.V(lambda: vc.tensor_tensor(den_ap, sl_e(wraw, 0), sl_e(wraw, 1), ALU.add))
        for e in range(2, E):
            b.V(lambda e=e: vc.tensor_tensor(den_ap, den_ap, sl_e(wraw, e), ALU.add))
        b.V(lambda: vc.tensor_scalar_add(den_ap, den_ap, 1e-20))
        b.V(lambda: vc.reciprocal(den_ap, den_ap))
        b.V(lambda: vc.tensor_scalar_mul(den_ap, den_ap, ROUTED_SCALE))
        for e in range(E):
            b.V(lambda e=e: vc.tensor_tensor(
                sl_e(wraw, e), sl_e(wraw, e), den_ap, ALU.mult))
        w_tok = VC.n
        for jj in range(nsub):
            sl = jj % 4
            co = sl * 512
            VC.wait(tn, max(pslot_free.get(sl, 0), w_tok))
            TN.inc(tn.transpose(psC[0:E, co:co + 128],
                                wraw[:, jj * E:(jj + 1) * E], ident[:]))
            TN.wait(vc)
            b.V(lambda jj=jj, co=co: vc.tensor_copy(
                wFrow[:, jj * 128:(jj + 1) * 128], psC[0:E, co:co + 128]))
            pslot_free[sl] = VC.n
        wfrow_done = VC.n
        if not hasattr(b, "_slot_last_mm"):
            b._slot_last_mm = {}
        b._slot_last_mm[0] = TN.n
        b._slot_last_mm[1] = TN.n
        pg.close()
        EWCOL = max(hc_n * I, ic_n * D)
        ew = [[p3.enter_context(nc.sbuf_tensor(f"ew{s}{m}", [128, EWCOL], BF16))
               for m in range(3)] for s in range(2)]

        est_free = {}
        ew_ready = {}

        def prefetch_expert(e, s):
            EWCOL = max(hc_n * I, ic_n * D)
            EWO.wait(gp)
            EW[s].wait(gp)
            VC.wait(gp, wfrow_done)
            TN.wait(gp, b._slot_last_mm.get(s, 0))
            for m in range(3):
                gp.dma_start(ew[s][m][:, 0:EWCOL],
                             ewb_h[e, m, :, 0:EWCOL]).then_inc(EW[s].h, 16)
                EW[s].n += 16
            ew_ready[s] = EW[s].n

        fin_read = {}
        x2_store = {}
        half = max(1, tsub_n // 2)
        b._x1dma = {}
        b._x1c_read = {}

        def emit_x1load(tc):
            slx = tc % 2
            t0l = tc * TC
            VC.wait(sy, b._x1c_read.get(slx, 0))
            X1W[0].wait(sy)
            X1W[1].wait(sy)
            X1R.wait(sy)
            X1R2.wait(sy)
            sy.dma_start(
                _ap2(x1c[slx], 0, [[D, half], [1, D]]),
                x1s_h[t0l:t0l + half * 128, :].rearrange("(j p) f -> p j f", p=128),
            ).then_inc(X1R.h, 16)
            X1R.n += 16
            if tsub_n > half:
                sy.dma_start(
                    _ap2(x1c[slx], half * D, [[D, tsub_n - half], [1, D]]),
                    x1s_h[t0l + half * 128:t0l + TC, :].rearrange(
                        "(j p) f -> p j f", p=128),
                ).then_inc(X1R2.h, 16)
                X1R2.n += 16
            b._x1dma[tc] = (X1R.n, X1R2.n)

        emit_x1load(0)
        b._prev_aval = 0
        for tc in range(n_tc):
            b.soft_barrier()
            t0 = tc * TC
            slx = tc % 2
            d1, d2 = b._x1dma[tc]
            if tc + 1 < n_tc:
                emit_x1load(tc + 1)

            if tc > 0:
                X1W[0].wait(sy)
                X1W[1].wait(sy)
                X1B[0].wait(sy)
                X1B[1].wait(sy)
                TN.wait(sy, b._prev_aval)
                XF.wait(sy)
                sy.dma_start_transpose(
                    bass.AP(x1f.ap().tensor, 0,
                            [[x1f.ap().ap[0][0], 128], [TC, di_n], [1, TC]]),
                    x1b_h[t0:t0 + TC, :]).then_inc(XF.h, 16)
                XF.n += 16
                b._xf_val = XF.n

            NU = nE * n_ts
            if not hasattr(b, "_slot_last_mm"):
                b._slot_last_mm = {}
            silu_val = {}
            gu_ready = {}
            ydone = {}
            aval_gate = {}
            aval_up = {}
            cval = {}
            if tc == 0:
                prefetch_expert(0, 0)
            prefetch_expert(1, 1)

            def emit_A(u):
                e, ts = divmod(u, n_ts)
                sl_w = (e % 2)
                tso = ts * TS
                wgB, wuB, _ = ew[sl_w]
                if u > 0:
                    SC.wait(tn, silu_val[u - 1])
                    VC.wait(tn, gu_ready[u - 1])
                else:
                    SC.wait(tn)
                EW[sl_w].wait(tn, ew_ready[sl_w])
                XF.wait(tn, b._xf_val)
                for ic in range(ic_n):
                    for hcx in range(hc_n):
                        ins = tn.matmul(psA[:, ic * TS:(ic + 1) * TS],
                                        wgB[:, hcx * I + ic * 128: hcx * I + (ic + 1) * 128],
                                        x1f[:, hcx * TC + tso: hcx * TC + tso + TS],
                                        start=(hcx == 0), stop=(hcx == hc_n - 1))
                TN.inc(ins)
                aval_gate[u] = TN.n
                for ic in range(ic_n):
                    for hcx in range(hc_n):
                        ins = tn.matmul(psB[:, ic * TS:(ic + 1) * TS],
                                        wuB[:, hcx * I + ic * 128: hcx * I + (ic + 1) * 128],
                                        x1f[:, hcx * TC + tso: hcx * TC + tso + TS],
                                        start=(hcx == 0), stop=(hcx == hc_n - 1))
                TN.inc(ins)
                aval_up[u] = TN.n

            def emit_B(u):
                e, ts = divmod(u, n_ts)
                tso = ts * TS
                p_ = u % 2
                TN.wait(sc, aval_gate[u])
                if u >= 2:
                    VC.wait(sc, gu_ready[u - 2])
                b.S(lambda p_=p_, n=ic_n * TS: sc.activation(
                    gs[p_][:, 0:n], psA[:, 0:n], AF.Silu))
                silu_val[u] = SC.n
                TN.wait(vc, aval_up[u])
                if e < E:
                    VC.wait(vc, wb_ready)
                    for ic in range(ic_n):
                        b.V(lambda ic=ic, e=e, tso=tso: vc.tensor_tensor(
                            guf[:, ic * TS:(ic + 1) * TS],
                            psB[:, ic * TS:(ic + 1) * TS],
                            wb[:, e * TC + tso: e * TC + tso + TS], ALU.mult))
                    SC.wait(vc, silu_val[u])
                    b.V(lambda p_=p_, n=ic_n * TS: vc.tensor_tensor(
                        gu[p_][:, 0:n], gs[p_][:, 0:n], guf[:, 0:n], ALU.mult))
                else:
                    SC.wait(vc, silu_val[u])
                    b.V(lambda p_=p_, n=ic_n * TS: vc.tensor_tensor(
                        gu[p_][:, 0:n], gs[p_][:, 0:n], psB[:, 0:n], ALU.mult))
                gu_ready[u] = VC.n

            def emit_C(u):
                e, ts = divmod(u, n_ts)
                sl_w = (e % 2)
                _, _, wdB = ew[sl_w]
                p_ = u % 2
                VC.wait(tn, gu_ready[u])
                if u >= 1:
                    VC.wait(tn, ydone[u - 1])
                for hcx in range(hc_n):
                    for ic in range(ic_n):
                        ins = tn.matmul(psC[:, hcx * TS:(hcx + 1) * TS],
                                        wdB[:, ic * D + hcx * 128: ic * D + (hcx + 1) * 128],
                                        gu[p_][:, ic * TS:(ic + 1) * TS],
                                        start=(ic == 0), stop=(ic == ic_n - 1))
                TN.inc(ins)
                cval[u] = TN.n
                b._slot_last_mm[sl_w] = TN.n

            def emit_D(u):
                e, ts = divmod(u, n_ts)
                tso = ts * TS
                TN.wait(vc, cval[u])
                if u < n_ts:
                    OUTS[0].wait(vc, x2_store.get(0, 0))
                    OUTS[1].wait(vc, x2_store.get(1, 0))
                yv = bass.AP(y_acc.ap().tensor, tso,
                             [[y_acc.ap().ap[0][0], 128], [TC, hc_n], [1, TS]])
                pv = bass.AP(psC.ap().tensor, 0,
                             [[psC.ap().ap[0][0], 128], [TS, hc_n], [1, TS]])
                if e == 0:
                    b.V(lambda: vc.tensor_copy(yv, pv))
                else:
                    b.V(lambda: vc.tensor_tensor(yv, yv, pv, ALU.add))
                ydone[u] = VC.n

            emit_A(0)
            for e in range(E):
                VC.wait(sy, wfrow_done)
                TN.wait(sy)
                sy.dma_start(wrowB1[0:1, :],
                             wFrow[e:e + 1, t0:t0 + TC]).then_inc(WR.h, 16)
                WR.n += 16
                WR.wait(tn)
                VC.wait(tn)
                for m in range((TC + 511) // 512):
                    w = min(512, TC - m * 512)
                    TN.inc(tn.matmul(psC[0:128, m * 512:m * 512 + w],
                                     ones1b[0:1, :],
                                     wrowB1[0:1, m * 512:m * 512 + w],
                                     start=True, stop=True))
                TN.wait(vc)
                b.V(lambda e=e: vc.tensor_copy(wb[:, e * TC:(e + 1) * TC],
                                               psC[:, 0:TC]))
            wb_ready = VC.n
            emit_B(0)

            for u in range(1, NU + 1):
                if u < NU:
                    emit_A(u)
                    emit_B(u)
                emit_C(u - 1)
                emit_D(u - 1)
                if u % n_ts == 0:
                    e_next = u // n_ts + 1
                    if e_next < nE:
                        prefetch_expert(e_next, e_next % 2)
                    elif tc + 1 < n_tc:
                        prefetch_expert(0, 0)
            ydone_v = ydone[NU - 1]
            b._prev_aval = aval_up[NU - 1]

            ntile3 = tsub_n
            r2_done = {}
            for j in range(ntile3):
                pst = psA if j % 2 == 0 else psB
                if j == 0:
                    VC.wait(tn, ydone_v)
                    SC.wait(tn)
                    VC.wait(tn)
                else:
                    VC.wait(tn, fin_read.get(j % 2, 0))
                for hcx in range(hc_n):
                    ins = tn.transpose(pst[:, hcx * 128:(hcx + 1) * 128],
                                       y_acc[:, hcx * TC + j * 128: hcx * TC + (j + 1) * 128],
                                       ident[:])
                TN.inc(ins)
                tdone = TN.n
                TN.wait(vc, tdone)
                X1R.wait(vc, d1)
                X1R2.wait(vc, d2)
                dst = x1c[slx][:, j * D:(j + 1) * D]
                b.V(lambda pst=pst, dst=dst: vc.tensor_add(dst, dst, pst[:, 0:D]))
                fin_read[j % 2] = VC.n
                r2_done[j] = VC.n
                nbs3 = (D + 511) // 512
                for c in range(nbs3):
                    w = min(512, D - c * 512)
                    b.V(lambda j=j, c=c, w=w, dst=dst, nbs3=nbs3: vc.bn_stats(
                        lnt38[:, 6 * (nbs3 * j + c):6 * (nbs3 * j + c) + 6],
                        dst[:, c * 512:c * 512 + w]))
            nbs3 = (D + 511) // 512
            mvo3 = 6 * nbs3 * ntile3
            for j in range(ntile3):
                b.V(lambda j=j, nbs3=nbs3, mvo3=mvo3: vc.bn_aggr(
                    lnt38[:, mvo3 + 2 * j:mvo3 + 2 * j + 2],
                    lnt38[:, 6 * nbs3 * j:6 * nbs3 * (j + 1)]))
            me3 = _ap2(lnt38, mvo3, [[2, ntile3]])
            va3 = _ap2(lnt38, mvo3 + 1, [[2, ntile3]])
            nm3 = lnt38[:, mvo3 + 2 * ntile3:mvo3 + 3 * ntile3]
            VC.wait(sc)
            b.S(lambda: sc.activation(va3, va3, AF.Ln, bias=consts[:, 0:1]))
            b.S(lambda: sc.activation(va3, va3, AF.Exp, scale=consts[:, 1:2]))
            SC.wait(vc)
            b.V(lambda: vc.tensor_tensor(nm3, me3, va3, ALU.mult))
            for j in range(ntile3):
                x2 = y_acc[:, j * D:(j + 1) * D]
                OUTS[j % 2].wait(vc, x2_store.get(j % 2, 0))
                b.V(lambda j=j, x2=x2, slx=slx: vc.tensor_scalar(
                    x2[:, 0:D], x1c[slx][:, j * D:(j + 1) * D],
                    lnt38[:, mvo3 + 1 + 2 * j:mvo3 + 2 + 2 * j], nm3[:, j:j + 1],
                    ALU.mult, ALU.subtract))
                VC.wait(sy)
                rows = slice(t0 + j * 128, t0 + (j + 1) * 128)
                sy.dma_start(out_h[rows, :], x2[:, 0:D]).then_inc(OUTS[j % 2].h, 16)
                OUTS[j % 2].n += 16
                x2_store[j % 2] = OUTS[j % 2].n
            b._x1c_read[slx] = VC.n

        OUTS[0].wait(sy)
        OUTS[1].wait(sy)

    nc.finalize()
    return nc


def _head_perm(H=16, KVH=4, Dh=64):
    rep = H // KVH
    qperm = []
    for p in range(H // 2):
        e = 2 * (p // rep) * rep + p % rep
        qperm += [e, e + rep]
    rows = []
    for h in qperm:
        rows += list(range(Dh * h, Dh * h + Dh))
    return qperm, np.array(rows)


def make_inputs(cfg, kw, batch_idx):
    qperm, qrows = _head_perm()
    rowperm = np.concatenate([qrows, np.arange(1024, 1536)])
    d = {}
    d["x"] = np.ascontiguousarray(kw["L0_attn_hidden_states"][batch_idx])
    d["wqkv"] = np.ascontiguousarray(np.asarray(kw["L0_attn_qkv_w"])[rowperm].T)
    d["bqkv"] = np.ascontiguousarray(np.asarray(kw["L0_attn_qkv_b"])[rowperm])
    d["scaling"] = np.ascontiguousarray(kw["L0_attn_scaling"])
    d["wo"] = np.ascontiguousarray(np.asarray(kw["L0_attn_o_w"])[:, qrows].T)
    d["bo"] = np.ascontiguousarray(kw["L0_attn_o_b"])
    d["wgate"] = np.ascontiguousarray(kw["L0_ffn_gate_w"])
    d["bgate"] = np.ascontiguousarray(kw["L0_ffn_gate_bias"])
    d["wg"] = np.ascontiguousarray(kw["L0_ffn_w_gate"])
    d["wu"] = np.ascontiguousarray(kw["L0_ffn_w_up"])
    d["wd"] = np.ascontiguousarray(kw["L0_ffn_w_down"])
    d["swg"] = np.ascontiguousarray(kw["L0_ffn_sw_gate"])
    d["swu"] = np.ascontiguousarray(kw["L0_ffn_sw_up"])
    d["swd"] = np.ascontiguousarray(kw["L0_ffn_sw_down"])
    return {k: np.ascontiguousarray(v).astype(np.float32) for k, v in d.items()}


def kernel(**inputs):
    cfg = full_cfg()
    nc = build_program(cfg)
    in_maps = [make_inputs(cfg, inputs, i) for i in range(8)]
    res = run_bass_kernel_spmd(nc, in_maps, core_ids=list(range(8)))
    out = np.stack([res.results[i]["out"] for i in range(8)], axis=0)
    return out.astype(np.float32)


# version 40
# speedup vs baseline: 1.0287x; 1.0023x over previous
import math
import numpy as np
from contextlib import ExitStack

import concourse.bass as bass
import concourse.mybir as mybir
from concourse.bass_utils import run_bass_kernel_spmd

F32 = mybir.dt.float32
BF16 = mybir.dt.bfloat16
AF = mybir.ActivationFunctionType
ALU = mybir.AluOpType
AX = mybir.AxisListType

R_SOFTPLUS_0 = 1.442695041
NEG_C = -60.0
LN_EPS = 1e-5
ROUTED_SCALE = 2.5


def full_cfg():
    return dict(S=4096, D=1024, H=16, KVH=4, Dh=64, E=8, I=512, SHI=1024,
                QC=512, TCW=512, TC=1024, TS=256)


class Sem:
    def __init__(self, nc, name, dma=False):
        self.h = nc.alloc_semaphore(name)
        self.n = 0
        self.step = 16 if dma else 1

    def inc(self, instr, k=1):
        instr.then_inc(self.h, k * self.step)
        self.n += k * self.step
        return self.n

    def wait(self, eng, val=None):
        v = self.n if val is None else val
        if v > 0:
            eng.wait_ge(self.h, v)


class B:
    def __init__(self, nc, cfg):
        self.nc = nc
        c = dict(cfg)
        self.c = c
        c["kt_n"] = c["S"] // 128
        c["rep"] = c["H"] // c["KVH"]
        c["G"] = c["KVH"]
        c["di_n"] = c["D"] // 128
        c["nQ"] = c["H"] // 2
        c["nK"] = c["KVH"] // 2
        c["do_n"] = c["nQ"] + 2 * c["nK"]
        c["DO"] = c["do_n"] * 128
        c["n_qc"] = c["S"] // c["QC"]
        c["n_tcw"] = c["S"] // c["TCW"]
        c["ic_n"] = c["I"] // 128
        c["hc_n"] = c["D"] // 128
        c["n_tc"] = c["S"] // c["TC"]
        c["n_ts"] = c["TC"] // c["TS"]
        c["tsub_n"] = c["TC"] // 128
        c["n_sh"] = c["SHI"] // c["I"]
        c["nE"] = c["E"] + c["n_sh"]
        assert c["E"] == 8 and c["Dh"] == 64 and c["nQ"] == c["di_n"]
        self.TN = Sem(nc, "TN")
        self.VC = Sem(nc, "VC")
        self.SC = Sem(nc, "SC")
        self.GP = Sem(nc, "GP")
        self._epoch = {}

    def V(self, instr_fn):
        self.VC.wait(self.nc.vector)
        return self.VC.inc(instr_fn())

    def S(self, instr_fn):
        self.SC.wait(self.nc.scalar)
        return self.SC.inc(instr_fn())

    def G(self, instr_fn):
        self.GP.wait(self.nc.gpsimd)
        return self.GP.inc(instr_fn())

    def soft_barrier(self):
        nc = self.nc
        for eng in (nc.tensor, nc.vector, nc.scalar, nc.gpsimd, nc.sync):
            for s in (self.TN, self.VC, self.SC, self.GP):
                s.wait(eng)

    def epoch(self, name):
        self._epoch[name] = set()

    def mm_start(self, name, col_start, col_end):
        banks = set(range(col_start // 512, (col_end + 511) // 512))
        st = self._epoch[name]
        new = banks - st
        st |= banks
        return bool(new)


def _ap2(t, off, dims):
    a = t.ap()
    return bass.AP(a.tensor, off, [[a.ap[0][0], a.ap[0][1]]] + dims)


def _emit_ln(b, src, dst, sqs, lnt, consts, D):
    nc = b.nc
    vc, sc = nc.vector, nc.scalar
    CEPS, CNH = 0, 1
    b.V(lambda: vc.reduce_sum(lnt[:, 0:1], src[:, 0:D], axis=AX.X))
    b.VC.wait(sc)
    b.S(lambda: sc.activation(sqs[:, 0:D], src[:, 0:D], AF.Square,
                              accum_out=lnt[:, 1:2]))
    b.SC.wait(vc)
    b.V(lambda: vc.tensor_scalar_mul(lnt[:, 0:1], lnt[:, 0:1], 1.0 / D))
    b.V(lambda: vc.tensor_scalar_mul(lnt[:, 1:2], lnt[:, 1:2], 1.0 / D))
    b.V(lambda: vc.tensor_tensor(lnt[:, 4:5], lnt[:, 0:1], lnt[:, 0:1], ALU.mult))
    b.V(lambda: vc.tensor_sub(lnt[:, 2:3], lnt[:, 1:2], lnt[:, 4:5]))
    b.VC.wait(sc)
    b.S(lambda: sc.activation(lnt[:, 3:4], lnt[:, 2:3], AF.Ln,
                              bias=consts[:, CEPS:CEPS + 1]))
    b.S(lambda: sc.activation(lnt[:, 3:4], lnt[:, 3:4], AF.Exp,
                              scale=consts[:, CNH:CNH + 1]))
    b.SC.wait(vc)
    b.V(lambda: vc.tensor_tensor(lnt[:, 4:5], lnt[:, 0:1], lnt[:, 3:4], ALU.mult))
    b.V(lambda: vc.tensor_scalar(dst[:, 0:D], src[:, 0:D], lnt[:, 3:4],
                                 lnt[:, 4:5], ALU.mult, ALU.subtract))


def build_program(cfg):
    nc = bass.Bass("TRN2", target_bir_lowering=False)
    b = B(nc, cfg)
    c = b.c
    S, D, H, KVH, Dh = c["S"], c["D"], c["H"], c["KVH"], c["Dh"]
    E, I, SHI = c["E"], c["I"], c["SHI"]
    QC, TCW, TC, TS = c["QC"], c["TCW"], c["TC"], c["TS"]
    kt_n, rep, G = c["kt_n"], c["rep"], c["G"]
    di_n, nQ, nK, do_n, DO = c["di_n"], c["nQ"], c["nK"], c["do_n"], c["DO"]
    n_qc, n_tcw = c["n_qc"], c["n_tcw"]
    ic_n, hc_n, n_tc, n_ts, tsub_n, nE = (
        c["ic_n"], c["hc_n"], c["n_tc"], c["n_ts"], c["tsub_n"], c["nE"])

    x_h = nc.dram_tensor("x", [S, D], F32, kind="ExternalInput").ap()
    wqkv_h = nc.dram_tensor("wqkv", [D, DO], F32, kind="ExternalInput").ap()
    bqkv_h = nc.dram_tensor("bqkv", [DO], F32, kind="ExternalInput").ap()
    scal_h = nc.dram_tensor("scaling", [Dh], F32, kind="ExternalInput").ap()
    wo_h = nc.dram_tensor("wo", [D, D], F32, kind="ExternalInput").ap()
    bo_h = nc.dram_tensor("bo", [D], F32, kind="ExternalInput").ap()
    wgate_h = nc.dram_tensor("wgate", [D, E], F32, kind="ExternalInput").ap()
    bgate_h = nc.dram_tensor("bgate", [E], F32, kind="ExternalInput").ap()
    wg_h = nc.dram_tensor("wg", [E, D, I], F32, kind="ExternalInput").ap()
    wu_h = nc.dram_tensor("wu", [E, D, I], F32, kind="ExternalInput").ap()
    wd_h = nc.dram_tensor("wd", [E, I, D], F32, kind="ExternalInput").ap()
    swg_h = nc.dram_tensor("swg", [D, SHI], F32, kind="ExternalInput").ap()
    swu_h = nc.dram_tensor("swu", [D, SHI], F32, kind="ExternalInput").ap()
    swd_h = nc.dram_tensor("swd", [SHI, D], F32, kind="ExternalInput").ap()
    out_h = nc.dram_tensor("out", [S, D], F32, kind="ExternalOutput").ap()
    x1s_h = nc.dram_tensor("x1s", [S, D], F32, kind="Internal").ap()
    x1b_h = nc.dram_tensor("x1b", [S, D], BF16, kind="Internal").ap()
    EWCOL0 = max((D // 128) * I, (I // 128) * D)
    nE0 = E + SHI // I
    ewb_h = nc.dram_tensor("ewb", [nE0, 3, 128, EWCOL0], BF16, kind="Internal").ap()

    ident = nc.alloc_sbuf_tensor("ident", [128, 128], F32)
    consts = nc.alloc_sbuf_tensor("consts", [128, 8], F32)
    CEPS, CNH, C1, CN1, CNC = 0, 1, 2, 3, 4
    bgate8 = nc.alloc_sbuf_tensor("bgate8", [E, 1], F32)
    ones1 = nc.alloc_sbuf_tensor("ones1", [1, 128], F32)
    ones1b = nc.alloc_sbuf_tensor("ones1b", [1, 128], BF16)
    attn_scope = ExitStack()
    _a = attn_scope.enter_context
    scal64 = _a(nc.sbuf_tensor("scal64", [64, 4], F32))
    scale128 = _a(nc.sbuf_tensor("scale128", [128, 1], F32))
    bq = _a(nc.sbuf_tensor("bq", [128, do_n], F32))
    beff = _a(nc.sbuf_tensor("beff", [128, do_n], F32))
    obB = _a(nc.sbuf_tensor("obB", [128, D], F32))
    QT = [_a(nc.sbuf_tensor(f"QT{p}", [128, S], BF16)) for p in range(nQ)]
    KTC = [_a(nc.sbuf_tensor(f"KTC{k}", [128, S], BF16)) for k in range(nK)]
    Vp = [_a(nc.sbuf_tensor(f"Vp{g}", [128, 128 * kt_n], BF16)) for g in range(G)]
    p0_scope = ExitStack()
    bo_row = p0_scope.enter_context(nc.sbuf_tensor("bo_row", [1, D], F32))

    psS = nc.alloc_psum_tensor("psS", [128, 2048], F32)
    psC = nc.alloc_psum_tensor("psC", [128, 2048], F32)

    class _PsView:

        def __init__(self, off):
            self.off = off

        def __getitem__(self, idx):
            ps_, fs = idx
            st = self.off + (fs.start or 0)
            sp = self.off + (fs.stop if fs.stop is not None else 1024)
            return psS[ps_, st:sp]

    psA = _PsView(0)
    psB = _PsView(1024)

    sy, tn, vc, sc, gp = nc.sync, nc.tensor, nc.vector, nc.scalar, nc.gpsimd
    TN, VC, SC, GP = b.TN, b.VC, b.SC, b.GP

    LD = Sem(nc, "LD", dma=True)
    WS = [Sem(nc, f"WS{i}", dma=True) for i in range(2)]
    X0 = [Sem(nc, f"X0{i}", dma=True) for i in range(4)]
    X0R = [Sem(nc, f"X0R{i}", dma=True) for i in range(2)]
    X1W = [Sem(nc, f"X1W{i}", dma=True) for i in range(2)]
    X1R = Sem(nc, "X1R", dma=True)
    X1B = [Sem(nc, f"X1B{i}", dma=True) for i in range(2)]
    XF = Sem(nc, "XF", dma=True)
    X0T = [Sem(nc, f"X0T{i}", dma=True) for i in range(4)]
    X1R2 = Sem(nc, "X1R2", dma=True)
    EW = [Sem(nc, f"EW{i}", dma=True) for i in range(2)]
    WR = Sem(nc, "WR", dma=True)
    EWI = [Sem(nc, f"EWI{i}", dma=True) for i in range(2)]
    EWO = Sem(nc, "EWO", dma=True)
    OUTS = [Sem(nc, f"OUTS{i}", dma=True) for i in range(2)]

    for k, v in {CEPS: LN_EPS, CNH: -0.5, C1: 1.0, CN1: -1.0, CNC: NEG_C}.items():
        b.V(lambda k=k, v=v: vc.memset(consts[:, k:k + 1], v))
    b.V(lambda: vc.memset(ident[:], 0.0))
    b.V(lambda: vc.memset(ones1[0:1, :], 1.0))
    b.V(lambda: vc.memset(ones1b[0:1, :], 1.0))
    VC.wait(gp)
    b.G(lambda: gp.affine_select(out=ident[:], in_=ident[:],
                                 compare_op=ALU.not_equal, fill=1.0, base=0,
                                 pattern=[[-1, 128]], channel_multiplier=1))
    for g in range(G):
        b.V(lambda g=g: vc.memset(Vp[g][:], 1.0))

    sy.dma_start(scal64[:, 0:1], scal_h.rearrange("(d a) -> d a", a=1)).then_inc(LD.h, 16)
    LD.n += 16
    for _c in range(do_n):
        sy.dma_start(bq[:, _c:_c + 1],
                     bqkv_h[_c * 128:(_c + 1) * 128].rearrange("(p a) -> p a", a=1),
                     allow_slow_non_contiguous=True).then_inc(LD.h, 16)
        LD.n += 16
    sy.dma_start(bo_row[0:1, :], bo_h.rearrange("(a d) -> a d", a=1)).then_inc(LD.h, 16)
    LD.n += 16
    sy.dma_start(bgate8[:, 0:1], bgate_h.rearrange("(e a) -> e a", a=1)).then_inc(LD.h, 16)
    LD.n += 16
    ld_p0 = LD.n

    LD.wait(vc, ld_p0)
    b.V(lambda: vc.tensor_scalar_mul(scal64[:, 1:2], scal64[:, 0:1], -1.0))
    b.V(lambda: vc.tensor_max(scal64[:, 1:2], scal64[:, 0:1], scal64[:, 1:2]))
    b.V(lambda: vc.tensor_single_scalar(scal64[:, 2:3], scal64[:, 0:1], 0.0, ALU.max))
    VC.wait(sc)
    b.S(lambda: sc.activation(scal64[:, 1:2], scal64[:, 1:2], AF.Exp,
                              scale=consts[0:64, CN1:CN1 + 1]))
    b.S(lambda: sc.activation(scal64[:, 1:2], scal64[:, 1:2], AF.Ln,
                              bias=consts[0:64, C1:C1 + 1]))
    SC.wait(vc)
    b.V(lambda: vc.tensor_add(scal64[:, 3:4], scal64[:, 2:3], scal64[:, 1:2]))
    b.V(lambda: vc.tensor_scalar_mul(scal64[:, 3:4], scal64[:, 3:4],
                                     R_SOFTPLUS_0 / math.sqrt(Dh)))
    b.V(lambda: vc.tensor_copy(scale128[0:64, 0:1], scal64[:, 3:4]))
    b.V(lambda: vc.tensor_copy(scale128[64:128, 0:1], scal64[:, 3:4]))
    LD.wait(vc, ld_p0)
    for dd in range(do_n):
        if dd < nQ:
            b.V(lambda dd=dd: vc.tensor_tensor(beff[:, dd:dd + 1], bq[:, dd:dd + 1],
                                               scale128[:, 0:1], ALU.mult))
        else:
            b.V(lambda dd=dd: vc.tensor_copy(beff[:, dd:dd + 1], bq[:, dd:dd + 1]))
    LD.wait(tn, ld_p0)
    VC.wait(tn)
    for m in range((D + 511) // 512):
        w = min(512, D - m * 512)
        TN.inc(tn.matmul(psC[0:128, m * 512:m * 512 + w], ones1[0:1, :],
                         bo_row[0:1, m * 512:m * 512 + w], start=True, stop=True))
    TN.wait(vc)
    b.V(lambda: vc.tensor_copy(obB[:, :], psC[:, 0:D]))

    nc.all_engine_barrier()
    p0_scope.close()

    ntiles = TCW // 128
    with ExitStack() as p1:
        wqkvb = p1.enter_context(nc.sbuf_tensor("wqkvb", [128, di_n * DO], BF16))
        stage = [p1.enter_context(nc.sbuf_tensor("wst0", [128, DO], F32))]
        x0b = [p1.enter_context(nc.sbuf_tensor(f"x0b{i}", [128, D], F32))
               for i in range(4)]
        x0c = [p1.enter_context(nc.sbuf_tensor(f"x0c{i}", [128, D], BF16))
               for i in range(4)]
        x0f = [p1.enter_context(nc.sbuf_tensor(f"x0f{i}", [128, di_n * TCW], BF16))
               for i in range(2)]
        vtmp = [p1.enter_context(nc.sbuf_tensor(f"vtmp{i}", [64, TCW], F32))
                for i in range(2)]

        for di in range(di_n):
            VC.wait(sy)
            sy.dma_start(stage[0][:], wqkv_h[di * 128:(di + 1) * 128, :]).then_inc(WS[0].h, 16)
            WS[0].n += 16
            WS[0].wait(vc)
            b.V(lambda di=di: vc.tensor_copy(wqkvb[:, di * DO:(di + 1) * DO],
                                             stage[0][:]))
        wqkv_ready = VC.n

        pslot_free = {}
        qkv_slot_free = {}
        vtmp_free = 0
        b._x0cast = {}
        b._x0t = {}
        b._qkv_last = {}
        for tc in range(n_tcw):
            for j in range(ntiles):
                s = (tc * ntiles + j) % 4
                VC.wait(sy, b._x0cast.get(s, 0))
                t0r = tc * TCW + j * 128
                sy.dma_start(x0b[s][:], x_h[t0r:t0r + 128, :]).then_inc(X0[s].h, 16)
                X0[s].n += 16
                X0[s].wait(vc)
                X0T[s].wait(vc, b._x0t.get(s, 0))
                b.V(lambda s=s: vc.tensor_copy(x0c[s][:, :], x0b[s][:, 0:D]))
                b._x0cast[s] = VC.n
                VC.wait(sy, b._x0cast[s])
                xsl = tc % 2
                TN.wait(sy, b._qkv_last.get(xsl, 0))
                X0T[s].wait(sy, b._x0t.get(s, 0))
                sy.dma_start_transpose(
                    bass.AP(x0f[xsl].ap().tensor, j * 128,
                            [[x0f[xsl].ap().ap[0][0], 128], [TCW, di_n], [1, 128]]),
                    x0c[s][:, :]).then_inc(X0T[s].h, 16)
                X0T[s].n += 16
                b._x0t[s] = X0T[s].n
            x0f_dma = tuple(X0T[i].n for i in range(4))
            xsl = tc % 2

            for dd in range(do_n):
                pst, off = [(psA, 0), (psA, 512), (psB, 0), (psB, 512)][dd % 4]
                pname = (id(pst), off)
                SC.wait(tn, qkv_slot_free.get(pname, 0))
                VC.wait(tn, wqkv_ready)
                for _i in range(4):
                    X0T[_i].wait(tn, x0f_dma[_i])
                for di in range(di_n):
                    ins = tn.matmul(pst[:, off:off + TCW],
                                    wqkvb[:, di * DO + dd * 128: di * DO + (dd + 1) * 128],
                                    x0f[xsl][:, di * TCW:(di + 1) * TCW],
                                    start=(di == 0), stop=(di == di_n - 1))
                TN.inc(ins)
                mm_done = TN.n
                TN.wait(sc, mm_done)
                if dd < nQ:
                    b.S(lambda dd=dd, tc=tc, pst=pst, off=off: sc.activation(
                        QT[dd][:, tc * TCW:(tc + 1) * TCW], pst[:, off:off + TCW],
                        AF.Identity, scale=scale128[:, 0:1], bias=beff[:, dd:dd + 1]))
                    qkv_slot_free[pname] = SC.n
                elif dd < nQ + nK:
                    kc = dd - nQ
                    cols = slice(tc * TCW, (tc + 1) * TCW)
                    b.S(lambda dd=dd, pst=pst, off=off, kc=kc, cols=cols:
                        sc.activation(KTC[kc][:, cols], pst[:, off:off + TCW],
                                      AF.Identity, bias=beff[:, dd:dd + 1]))
                    qkv_slot_free[pname] = SC.n
                else:
                    vcc = dd - nQ - nK
                    TN.wait(sc, vtmp_free)
                    b.S(lambda dd=dd, pst=pst, off=off: sc.activation(
                        vtmp[0][0:64, :], pst[0:64, off:off + TCW], AF.Identity,
                        bias=beff[0:64, dd:dd + 1]))
                    b.S(lambda dd=dd, pst=pst, off=off: sc.activation(
                        vtmp[1][0:64, :], pst[64:128, off:off + TCW], AF.Identity,
                        bias=beff[64:128, dd:dd + 1]))
                    qkv_slot_free[pname] = SC.n
                    vevac = SC.n
                    for ktl in range(ntiles):
                        for gl in range(2):
                            g = 2 * vcc + gl
                            kt = tc * ntiles + ktl
                            sl = (ktl * 2 + gl) % 4
                            co = sl * 512
                            SC.wait(tn, vevac)
                            VC.wait(tn, pslot_free.get(sl, 0))
                            TN.inc(tn.transpose(psC[:, co:co + 64],
                                                vtmp[gl][0:64, ktl * 128:(ktl + 1) * 128],
                                                ident[0:64, 0:64]))
                            TN.wait(vc)
                            vdo = 128 * kt + (64 if g % 2 else 0)
                            b.V(lambda g=g, vdo=vdo, co=co: vc.tensor_copy(
                                Vp[g][:, vdo:vdo + 64],
                                psC[:, co:co + 64]))
                            pslot_free[sl] = VC.n
                    vtmp_free = TN.n
            b._qkv_last[xsl] = TN.n

    X0[0].wait(sy)
    X0[1].wait(sy)
    WS[0].wait(sy)
    nc.all_engine_barrier()

    with ExitStack() as p2:
        attnF = [p2.enter_context(nc.sbuf_tensor(f"attnF{p}", [128, QC], BF16))
                 for p in range(nQ)]
        woB = p2.enter_context(nc.sbuf_tensor("woB", [128, di_n * D], BF16))
        pb = [p2.enter_context(nc.sbuf_tensor(f"pb{i}", [128, 2 * QC], BF16))
              for i in range(4)]
        mrgP = p2.enter_context(nc.sbuf_tensor("mrgP", [128, QC], F32))
        mrgS = p2.enter_context(nc.sbuf_tensor("mrgS", [128, QC], F32))
        mrgR = p2.enter_context(nc.sbuf_tensor("mrgR", [128, QC], F32))
        CW0 = min(1024, max(hc_n * I, ic_n * D))
        cstage = [p2.enter_context(nc.sbuf_tensor("cstage0", [128, CW0], F32))]
        x1bb = [p2.enter_context(nc.sbuf_tensor(f"x1bb{i}", [128, D], BF16))
                for i in range(2)]
        x0r = [p2.enter_context(nc.sbuf_tensor(f"x0r{i}", [128, D], F32))
               for i in range(2)]
        r1_all = p2.enter_context(nc.sbuf_tensor("r1_all", [128, (QC // 128) * D], F32))
        x1t = [p2.enter_context(nc.sbuf_tensor(f"x1t{i}", [128, D], F32))
               for i in range(2)]
        lnt8 = p2.enter_context(nc.sbuf_tensor("lnt8", [128, 64], F32))
        ewcast = p2.enter_context(nc.sbuf_tensor("ewcast", [128, CW0], BF16))

        for di in range(di_n):
            VC.wait(sy)
            sy.dma_start(cstage[0][:, 0:D],
                         wo_h[di * 128:(di + 1) * 128, :]).then_inc(WS[0].h, 16)
            WS[0].n += 16
            WS[0].wait(vc)
            b.V(lambda di=di: vc.tensor_copy(woB[:, di * D:(di + 1) * D],
                                             cstage[0][:, 0:D]))
        wo_ready = VC.n
        VC_after_wo = VC.n

        EWCOL = max(hc_n * I, ic_n * D)
        CW = min(1024, EWCOL)
        cast_jobs = []
        for e in range(nE):
            if e < E:
                srcs = [wg_h[e], wu_h[e], wd_h[e]]
            else:
                hh = e - E
                srcs = [swg_h[:, hh * I:(hh + 1) * I],
                        swu_h[:, hh * I:(hh + 1) * I],
                        swd_h[hh * I:(hh + 1) * I, :]]
            for m in range(3):
                ncols = I if m < 2 else D
                cpc = CW // ncols
                for k in range(EWCOL // CW):
                    cast_jobs.append((e, m, k, srcs[m], ncols, cpc))
        b._cast_issue = 0
        b._cast_done = 0
        b._cast_gp = {}

        def issue_cast_dma():
            if b._cast_issue >= len(cast_jobs):
                return
            i = b._cast_issue
            b._cast_issue += 1
            e, m, k, src, ncols, cpc = cast_jobs[i]
            sl = 0
            if i == 0:
                VC.wait(gp, VC_after_wo)
            gp.dma_start(
                cstage[sl].ap().rearrange("p (c i) -> p c i", c=cpc)[:, :, 0:ncols],
                src.rearrange("(c p) i -> p c i", p=128)[:, k * cpc:(k + 1) * cpc, :],
            ).then_inc(EWI[sl].h, 16)
            EWI[sl].n += 16

        def emit_cast_chunk():
            if b._cast_done >= b._cast_issue:
                return
            i = b._cast_done
            b._cast_done += 1
            e, m, k, src, ncols, cpc = cast_jobs[i]
            sl = 0
            EWI[sl].wait(gp, (i + 1) * 16)
            EWO.wait(gp)
            b.G(lambda sl=sl, n=cpc * ncols: gp.tensor_copy(
                ewcast[:, 0:n], cstage[sl][:, 0:n]))
            b._cast_gp[sl] = GP.n
            GP.wait(gp, b._cast_gp[sl])
            gp.dma_start(ewb_h[e, m, :, k * CW:(k + 1) * CW],
                         ewcast[:, 0:CW]).then_inc(EWO.h, 16)
            EWO.n += 16
        issue_cast_dma()

        b._expslot = {}
        b._pbslot = {}
        b._psC_free_v = 0
        b._psC_free4 = [0, 0, 0, 0]
        b._oproj_tn = 0
        b._attn_div_v = 0
        b._x1st = {}
        b._x1bt = {}
        b._fin_v = {}
        oslot_free = {}
        pending_fin = None

        def emit_fin(st):
            p, pvlast, first_of_qc = st
            TN.wait(vc, pvlast)
            if first_of_qc:
                TN.wait(vc, b._oproj_tn)
            b.V(lambda: vc.tensor_copy(mrgP[0:64, :], psC[0:64, 0:QC]))
            b.V(lambda: vc.tensor_copy(mrgS[64:128, :], psC[64:128, 0:QC]))
            b._psC_free4[0] = VC.n
            b.V(lambda: vc.tensor_add(mrgP[0:64, :], mrgP[0:64, :],
                                      psC[0:64, 512:512 + QC]))
            b.V(lambda: vc.tensor_add(mrgS[64:128, :], mrgS[64:128, :],
                                      psC[64:128, 512:512 + QC]))
            b._psC_free4[1] = VC.n
            b.V(lambda: vc.tensor_copy(mrgP[64:128, :], psC[64:128, 1024:1024 + QC]))
            b.V(lambda: vc.tensor_copy(mrgS[0:64, :], psC[0:64, 1024:1024 + QC]))
            b._psC_free4[2] = VC.n
            b.V(lambda: vc.tensor_add(mrgP[64:128, :], mrgP[64:128, :],
                                      psC[64:128, 1536:1536 + QC]))
            b.V(lambda: vc.tensor_add(mrgS[0:64, :], mrgS[0:64, :],
                                      psC[0:64, 1536:1536 + QC]))
            b._psC_free4[3] = VC.n
            b._psC_free_v = VC.n
            b.V(lambda: vc.reciprocal(mrgR[0:64, :], mrgS[64:128, :]))
            b.V(lambda: vc.reciprocal(mrgR[64:128, :], mrgS[0:64, :]))
            b.V(lambda p=p: vc.tensor_tensor(
                attnF[p][0:64, :], mrgP[0:64, :], mrgR[0:64, :], ALU.mult))
            b.V(lambda p=p: vc.tensor_tensor(
                attnF[p][64:128, :], mrgP[64:128, :], mrgR[64:128, :], ALU.mult))
            b._attn_div_v = VC.n
            b._fin_v[p] = VC.n

        pending_tail = None
        ntile = QC // 128
        nbs = (D + 511) // 512
        mvo = 6 * nbs * ntile

        def emit_ln_tail(q0t):
            for j in range(ntile):
                b.V(lambda j=j: vc.bn_aggr(
                    lnt8[:, mvo + 2 * j:mvo + 2 * j + 2],
                    lnt8[:, 6 * nbs * j:6 * nbs * (j + 1)]))
            me = _ap2(lnt8, mvo, [[2, ntile]])
            va = _ap2(lnt8, mvo + 1, [[2, ntile]])
            nm = lnt8[:, mvo + 2 * ntile:mvo + 3 * ntile]
            VC.wait(sc)
            b.S(lambda: sc.activation(va, va, AF.Ln, bias=consts[:, 0:1]))
            b.S(lambda: sc.activation(va, va, AF.Exp, scale=consts[:, 1:2]))
            SC.wait(vc)
            b.V(lambda: vc.tensor_tensor(nm, me, va, ALU.mult))
            for j in range(ntile):
                x1 = x1t[j % 2]
                X1W[j % 2].wait(vc, b._x1st.get(j % 2, 0))
                b.V(lambda j=j, x1=x1: vc.tensor_scalar(
                    x1[:, 0:D], r1_all[:, j * D:(j + 1) * D],
                    lnt8[:, mvo + 1 + 2 * j:mvo + 2 + 2 * j], nm[:, j:j + 1],
                    ALU.mult, ALU.subtract))
                X1B[j % 2].wait(vc, b._x1bt.get(j % 2, 0))
                b.V(lambda j=j, x1=x1: vc.tensor_copy(x1bb[j % 2][:, :],
                                                      x1[:, 0:D]))
                VC.wait(sy)
                rows = slice(q0t + j * 128, q0t + (j + 1) * 128)
                sy.dma_start(x1s_h[rows, :], x1[:]).then_inc(X1W[j % 2].h, 16)
                X1W[j % 2].n += 16
                sy.dma_start(x1b_h[rows, :], x1bb[j % 2][:, :]).then_inc(X1B[j % 2].h, 16)
                X1B[j % 2].n += 16
                b._x1st[j % 2] = X1W[j % 2].n
                b._x1bt[j % 2] = X1B[j % 2].n

        for qc in range(n_qc):
            q0 = qc * QC
            NB = nQ * kt_n
            exp_val = {}
            scores_done = {}
            for v in range(NB + 4):
                if v >= 4:
                    u = v - 4
                    pP, ktP = divmod(u, kt_n)
                    if ktP == 0 and pending_fin is not None:
                        emit_fin(pending_fin)
                        pending_fin = None
                    kcP = pP // rep
                    gE, gO = 2 * kcP, 2 * kcP + 1
                    slp = u % 4
                    vo = ktP * 128
                    st = (ktP == 0)
                    sp = (ktP == kt_n - 1)
                    SC.wait(tn, exp_val[u])
                    if st:
                        VC.wait(tn, b._psC_free4[0])
                    tn.matmul(psC[0:128, 0:QC], Vp[gE][0:64, vo:vo + 128],
                              pb[slp][0:64, 0:QC], start=st, stop=sp,
                              tile_position=(0, 0))
                    if st:
                        VC.wait(tn, b._psC_free4[1])
                    tn.matmul(psC[0:128, 512:512 + QC],
                              Vp[gE][64:128, vo:vo + 128],
                              pb[slp][64:128, 0:QC], start=st, stop=sp,
                              tile_position=(64, 0))
                    if st:
                        VC.wait(tn, b._psC_free4[2])
                    tn.matmul(psC[0:128, 1024:1024 + QC],
                              Vp[gO][0:64, vo:vo + 128],
                              pb[slp][0:64, QC:2 * QC], start=st, stop=sp,
                              tile_position=(0, 0))
                    if st:
                        VC.wait(tn, b._psC_free4[3])
                    TN.inc(tn.matmul(psC[0:128, 1536:1536 + QC],
                                     Vp[gO][64:128, vo:vo + 128],
                                     pb[slp][64:128, QC:2 * QC], start=st,
                                     stop=sp, tile_position=(64, 0)))
                    b._pbslot[slp] = TN.n
                    if sp:
                        pending_fin = (pP, TN.n, pP == 0)
                if v < NB:
                    pS, ktS = divmod(v, kt_n)
                    sl = v % 2
                    so = sl * 1024
                    ko = ktS * 128
                    kcS = pS // rep
                    SC.wait(tn, b._expslot.get(sl, 0))
                    tn.matmul(psS[0:128, so:so + QC],
                              KTC[kcS][0:64, ko:ko + 128],
                              QT[pS][0:64, q0:q0 + QC],
                              start=True, stop=True, tile_position=(0, 0))
                    TN.inc(tn.matmul(psS[0:128, so + QC:so + 2 * QC],
                                     KTC[kcS][64:128, ko:ko + 128],
                                     QT[pS][64:128, q0:q0 + QC],
                                     start=True, stop=True,
                                     tile_position=(64, 0)))
                    scores_done[v] = TN.n
                if 1 <= v <= NB:
                    ke = v - 1
                    sl = ke % 2
                    TN.wait(sc, scores_done[ke])
                    TN.wait(sc, b._pbslot.get(ke % 4, 0))
                    SC.inc(sc.activation(pb[ke % 4][:, 0:2 * QC],
                                         psS[:, sl * 1024:sl * 1024 + 2 * QC],
                                         AF.Exp, bias=consts[:, CNC:CNC + 1]))
                    exp_val[ke] = SC.n
                    b._expslot[sl] = SC.n
                if v == 6 and pending_tail is not None:
                    emit_ln_tail(pending_tail)
                    pending_tail = None
                if v % 8 == 4:
                    emit_cast_chunk()
                    issue_cast_dma()
            if pending_fin is not None:
                emit_fin(pending_fin)
                pending_fin = None

            oNC = (D + 511) // 512
            for j in range(ntile):
                sq = j % 2
                po = 1024 * sq
                rows = slice(q0 + j * 128, q0 + (j + 1) * 128)
                VC.wait(sy, oslot_free.get(("x", sq), 0))
                sy.dma_start(x0r[sq][:, 0:D], x_h[rows, :]).then_inc(X0R[sq].h, 16)
                X0R[sq].n += 16
                VC.wait(tn, max(wo_ready, b._psC_free_v,
                                oslot_free.get(("ps", sq), 0)))
                for m in range(oNC):
                    w = min(512, D - m * 512)
                    for di in range(di_n):
                        VC.wait(tn, b._fin_v.get(di, 0))
                        ins = tn.matmul(psC[0:128, po + m * 512:po + m * 512 + w],
                                        attnF[di][:, j * 128:(j + 1) * 128],
                                        woB[:, di * D + m * 512: di * D + m * 512 + w],
                                        start=(di == 0), stop=(di == di_n - 1))
                TN.inc(ins)
                odone = TN.n
                b._oproj_tn = TN.n
                TN.wait(vc, odone)
                X0R[sq].wait(vc)
                dst = r1_all[:, j * D:(j + 1) * D]
                b.V(lambda po=po, sq=sq, dst=dst: vc.tensor_add(
                    dst, psC[0:128, po:po + D], x0r[sq][:, 0:D]))
                b.V(lambda dst=dst: vc.tensor_add(dst, dst, obB[:, :]))
                oslot_free[("x", sq)] = VC.n
                oslot_free[("ps", sq)] = VC.n
                nbs = (D + 511) // 512
                for c in range(nbs):
                    w = min(512, D - c * 512)
                    b.V(lambda j=j, c=c, w=w, dst=dst, nbs=nbs: vc.bn_stats(
                        lnt8[:, 6 * (nbs * j + c):6 * (nbs * j + c) + 6],
                        dst[:, c * 512:c * 512 + w]))
            b._psC_free_v = VC.n
            b._psC_free4 = [VC.n] * 4
            pending_tail = q0

        if pending_tail is not None:
            emit_ln_tail(pending_tail)
            pending_tail = None
        while b._cast_done < len(cast_jobs):
            emit_cast_chunk()
            issue_cast_dma()
        X0R[0].wait(sy)
        X0R[1].wait(sy)
        X1B[0].wait(sy)
        X1B[1].wait(sy)
        WS[0].wait(sy)
        EWI[0].wait(sy)
        EWI[1].wait(sy)
        EWO.wait(sy)

    nc.all_engine_barrier()
    attn_scope.close()

    with ExitStack() as p3:
        x1c = [p3.enter_context(nc.sbuf_tensor(f"x1c{i}", [128, tsub_n * D], F32))
               for i in range(2)]
        x1f = p3.enter_context(nc.sbuf_tensor("x1f", [128, di_n * TC], BF16))
        wb = p3.enter_context(nc.sbuf_tensor("wb", [128, E * TC], BF16))
        y_acc = p3.enter_context(nc.sbuf_tensor("y_acc", [128, hc_n * TC], F32))
        gs = [p3.enter_context(nc.sbuf_tensor(f"gs{i}", [128, ic_n * TS], BF16))
              for i in range(2)]
        guf = p3.enter_context(nc.sbuf_tensor("guf", [128, ic_n * TS], BF16))
        gu = [p3.enter_context(nc.sbuf_tensor(f"gu{i}", [128, ic_n * TS], BF16))
              for i in range(2)]
        wgateB = p3.enter_context(nc.sbuf_tensor("wgateB", [128, di_n * E], BF16))
        wgstage = p3.enter_context(nc.sbuf_tensor("wgstage", [128, di_n * E], F32))
        nsub = S // 128
        wFrow = p3.enter_context(nc.sbuf_tensor("wFrow", [E, S], BF16))
        wrowB1 = p3.enter_context(nc.sbuf_tensor("wrowB1", [1, TC], BF16))
        lnt38 = p3.enter_context(nc.sbuf_tensor("lnt38", [128, 128], F32))
        pg = ExitStack()
        _g = pg.enter_context
        sfc_f = _g(nc.sbuf_tensor("sfc_f", [E, TC], F32))
        st_t = _g(nc.sbuf_tensor("st_t", [128, nsub * E], F32))
        gsum = _g(nc.sbuf_tensor("gsum", [128, nsub * 4], F32))
        cntg = _g(nc.sbuf_tensor("cntg", [128, nsub * 4], F32))
        selg = _g(nc.sbuf_tensor("selg", [128, nsub * 4], F32))
        tmpe = _g(nc.sbuf_tensor("tmpe", [128, nsub * E], F32))
        cnte = _g(nc.sbuf_tensor("cnte", [128, nsub * E], F32))
        wraw = _g(nc.sbuf_tensor("wraw", [128, nsub * E], F32))
        scr8 = _g(nc.sbuf_tensor("scr8", [128, nsub * E], F32))
        den = _g(nc.sbuf_tensor("den", [128, nsub], F32))

        sy.dma_start(wgstage.ap().rearrange("p (c e) -> p c e", c=di_n),
                     wgate_h.rearrange("(c p) e -> p c e", p=128)).then_inc(LD.h, 16)
        LD.n += 16
        LD.wait(vc)
        b.V(lambda: vc.tensor_copy(wgateB[:, :], wgstage[:, :]))
        wgate_ready = VC.n

        pslot_free = {}
        b._gate_sc = 0
        for tcg in list(range(1, n_tc)) + [0]:
            tg0 = tcg * TC
            X1B[0].wait(sy)
            X1B[1].wait(sy)
            XF.wait(sy)
            TN.wait(sy)
            sy.dma_start_transpose(
                bass.AP(x1f.ap().tensor, 0,
                        [[x1f.ap().ap[0][0], 128], [TC, di_n], [1, TC]]),
                x1b_h[tg0:tg0 + TC, :]).then_inc(XF.h, 16)
            XF.n += 16
            b._xf_val = XF.n
            XF.wait(tn, b._xf_val)
            VC.wait(tn, wgate_ready)
            SC.wait(tn)
            for di in range(di_n):
                for m in range((TC + 511) // 512):
                    w = min(512, TC - m * 512)
                    ins = tn.matmul(psA[0:E, m * 512:m * 512 + w],
                                    wgateB[:, di * E:(di + 1) * E],
                                    x1f[:, di * TC + m * 512: di * TC + m * 512 + w],
                                    start=(di == 0), stop=(di == di_n - 1))
            TN.inc(ins)
            TN.wait(sc)
            b.S(lambda: sc.activation(sfc_f[:, :], psA[0:E, 0:TC], AF.Sigmoid))
            b._gate_sc = SC.n
            SC.wait(vc)
            LD.wait(vc)
            b.V(lambda: vc.tensor_scalar_add(sfc_f[:, :], sfc_f[:, :], bgate8[:, 0:1]))
            sfc_ready = VC.n
            for j in range(tsub_n):
                jj = tcg * tsub_n + j
                sl = jj % 4
                co = sl * 512
                VC.wait(tn, max(pslot_free.get(sl, 0), sfc_ready))
                TN.inc(tn.transpose(psC[0:128, co:co + E],
                                    sfc_f[:, j * 128:(j + 1) * 128],
                                    ident[0:E, 0:E]))
                TN.wait(vc)
                b.V(lambda jj=jj, co=co: vc.tensor_copy(
                    st_t[:, jj * E:(jj + 1) * E], psC[:, co:co + E]))
                pslot_free[sl] = VC.n

        def sl_e(tens, e, n=E):
            return _ap2(tens, e, [[n, nsub]])

        for g4 in range(4):
            b.V(lambda g4=g4: vc.tensor_tensor(sl_e(gsum, g4, 4),
                                               sl_e(st_t, 2 * g4),
                                               sl_e(st_t, 2 * g4 + 1), ALU.add))
        for g4 in range(4):
            others = [x for x in range(4) if x != g4]
            b.V(lambda g4=g4, o=others[0]: vc.tensor_tensor(
                sl_e(cntg, g4, 4), sl_e(gsum, o, 4), sl_e(gsum, g4, 4), ALU.is_gt))
            for o in others[1:]:
                b.V(lambda g4=g4, o=o: vc.tensor_tensor(
                    sl_e(selg, g4, 4), sl_e(gsum, o, 4), sl_e(gsum, g4, 4), ALU.is_gt))
                b.V(lambda g4=g4: vc.tensor_tensor(
                    sl_e(cntg, g4, 4), sl_e(cntg, g4, 4), sl_e(selg, g4, 4), ALU.add))
        for g4 in range(4):
            b.V(lambda g4=g4: vc.tensor_single_scalar(
                sl_e(selg, g4, 4), sl_e(cntg, g4, 4), 2.0, ALU.is_lt))
        for e in range(E):
            b.V(lambda e=e: vc.tensor_tensor(
                sl_e(tmpe, e), sl_e(st_t, e), sl_e(selg, e // 2, 4), ALU.mult))
        for e in range(E):
            others = [x for x in range(E) if x != e]
            b.V(lambda e=e, o=others[0]: vc.tensor_tensor(
                sl_e(cnte, e), sl_e(tmpe, o), sl_e(tmpe, e), ALU.is_gt))
            for o in others[1:]:
                b.V(lambda e=e, o=o: vc.tensor_tensor(
                    sl_e(scr8, e), sl_e(tmpe, o), sl_e(tmpe, e), ALU.is_gt))
                b.V(lambda e=e: vc.tensor_tensor(
                    sl_e(cnte, e), sl_e(cnte, e), sl_e(scr8, e), ALU.add))
        for e in range(E):
            b.V(lambda e=e: vc.tensor_single_scalar(
                sl_e(scr8, e), sl_e(cnte, e), 2.0, ALU.is_lt))
            b.V(lambda e=e: vc.tensor_single_scalar(
                sl_e(cnte, e), sl_e(tmpe, e), 0.0, ALU.is_gt))
            b.V(lambda e=e: vc.tensor_tensor(
                sl_e(scr8, e), sl_e(scr8, e), sl_e(cnte, e), ALU.mult))
            b.V(lambda e=e: vc.tensor_tensor(
                sl_e(wraw, e), sl_e(scr8, e), sl_e(st_t, e), ALU.mult))
        den_ap = _ap2(den, 0, [[1, nsub]])
        b.V(lambda: vc.tensor_tensor(den_ap, sl_e(wraw, 0), sl_e(wraw, 1), ALU.add))
        for e in range(2, E):
            b.V(lambda e=e: vc.tensor_tensor(den_ap, den_ap, sl_e(wraw, e), ALU.add))
        b.V(lambda: vc.tensor_scalar_add(den_ap, den_ap, 1e-20))
        b.V(lambda: vc.reciprocal(den_ap, den_ap))
        b.V(lambda: vc.tensor_scalar_mul(den_ap, den_ap, ROUTED_SCALE))
        for e in range(E):
            b.V(lambda e=e: vc.tensor_tensor(
                sl_e(wraw, e), sl_e(wraw, e), den_ap, ALU.mult))
        w_tok = VC.n
        for jj in range(nsub):
            sl = jj % 4
            co = sl * 512
            VC.wait(tn, max(pslot_free.get(sl, 0), w_tok))
            TN.inc(tn.transpose(psC[0:E, co:co + 128],
                                wraw[:, jj * E:(jj + 1) * E], ident[:]))
            TN.wait(vc)
            b.V(lambda jj=jj, co=co: vc.tensor_copy(
                wFrow[:, jj * 128:(jj + 1) * 128], psC[0:E, co:co + 128]))
            pslot_free[sl] = VC.n
        wfrow_done = VC.n
        if not hasattr(b, "_slot_last_mm"):
            b._slot_last_mm = {}
        b._slot_last_mm[0] = TN.n
        b._slot_last_mm[1] = TN.n
        pg.close()
        EWCOL = max(hc_n * I, ic_n * D)
        ew = [[p3.enter_context(nc.sbuf_tensor(f"ew{s}{m}", [128, EWCOL], BF16))
               for m in range(3)] for s in range(2)]

        est_free = {}
        ew_ready = {}

        def prefetch_expert(e, s):
            EWCOL = max(hc_n * I, ic_n * D)
            EWO.wait(gp)
            EW[s].wait(gp)
            VC.wait(gp, wfrow_done)
            TN.wait(gp, b._slot_last_mm.get(s, 0))
            for m in range(3):
                gp.dma_start(ew[s][m][:, 0:EWCOL],
                             ewb_h[e, m, :, 0:EWCOL]).then_inc(EW[s].h, 16)
                EW[s].n += 16
            ew_ready[s] = EW[s].n

        fin_read = {}
        x2_store = {}
        half = max(1, tsub_n // 2)
        b._x1dma = {}
        b._x1c_read = {}

        def emit_x1load(tc):
            slx = tc % 2
            t0l = tc * TC
            VC.wait(sy, b._x1c_read.get(slx, 0))
            X1W[0].wait(sy)
            X1W[1].wait(sy)
            X1R.wait(sy)
            X1R2.wait(sy)
            sy.dma_start(
                _ap2(x1c[slx], 0, [[D, half], [1, D]]),
                x1s_h[t0l:t0l + half * 128, :].rearrange("(j p) f -> p j f", p=128),
            ).then_inc(X1R.h, 16)
            X1R.n += 16
            if tsub_n > half:
                sy.dma_start(
                    _ap2(x1c[slx], half * D, [[D, tsub_n - half], [1, D]]),
                    x1s_h[t0l + half * 128:t0l + TC, :].rearrange(
                        "(j p) f -> p j f", p=128),
                ).then_inc(X1R2.h, 16)
                X1R2.n += 16
            b._x1dma[tc] = (X1R.n, X1R2.n)

        emit_x1load(0)
        b._prev_aval = 0
        for tc in range(n_tc):
            b.soft_barrier()
            t0 = tc * TC
            slx = tc % 2
            d1, d2 = b._x1dma[tc]
            if tc + 1 < n_tc:
                emit_x1load(tc + 1)

            if tc > 0:
                X1W[0].wait(sy)
                X1W[1].wait(sy)
                X1B[0].wait(sy)
                X1B[1].wait(sy)
                TN.wait(sy, b._prev_aval)
                XF.wait(sy)
                sy.dma_start_transpose(
                    bass.AP(x1f.ap().tensor, 0,
                            [[x1f.ap().ap[0][0], 128], [TC, di_n], [1, TC]]),
                    x1b_h[t0:t0 + TC, :]).then_inc(XF.h, 16)
                XF.n += 16
                b._xf_val = XF.n

            NU = nE * n_ts
            if not hasattr(b, "_slot_last_mm"):
                b._slot_last_mm = {}
            silu_val = {}
            gu_ready = {}
            ydone = {}
            aval_gate = {}
            aval_up = {}
            cval = {}
            if tc == 0:
                prefetch_expert(0, 0)
            prefetch_expert(1, 1)

            def emit_A(u):
                e, ts = divmod(u, n_ts)
                sl_w = (e % 2)
                tso = ts * TS
                wgB, wuB, _ = ew[sl_w]
                if u > 0:
                    SC.wait(tn, silu_val[u - 1])
                    VC.wait(tn, gu_ready[u - 1])
                else:
                    SC.wait(tn)
                EW[sl_w].wait(tn, ew_ready[sl_w])
                XF.wait(tn, b._xf_val)
                for ic in range(ic_n):
                    for hcx in range(hc_n):
                        ins = tn.matmul(psA[:, ic * TS:(ic + 1) * TS],
                                        wgB[:, hcx * I + ic * 128: hcx * I + (ic + 1) * 128],
                                        x1f[:, hcx * TC + tso: hcx * TC + tso + TS],
                                        start=(hcx == 0), stop=(hcx == hc_n - 1))
                TN.inc(ins)
                aval_gate[u] = TN.n
                for ic in range(ic_n):
                    for hcx in range(hc_n):
                        ins = tn.matmul(psB[:, ic * TS:(ic + 1) * TS],
                                        wuB[:, hcx * I + ic * 128: hcx * I + (ic + 1) * 128],
                                        x1f[:, hcx * TC + tso: hcx * TC + tso + TS],
                                        start=(hcx == 0), stop=(hcx == hc_n - 1))
                TN.inc(ins)
                aval_up[u] = TN.n

            def emit_B(u):
                e, ts = divmod(u, n_ts)
                tso = ts * TS
                p_ = u % 2
                TN.wait(sc, aval_gate[u])
                if u >= 2:
                    VC.wait(sc, gu_ready[u - 2])
                b.S(lambda p_=p_, n=ic_n * TS: sc.activation(
                    gs[p_][:, 0:n], psA[:, 0:n], AF.Silu))
                silu_val[u] = SC.n
                TN.wait(vc, aval_up[u])
                if e < E:
                    VC.wait(vc, wb_ready)
                    for ic in range(ic_n):
                        b.V(lambda ic=ic, e=e, tso=tso: vc.tensor_tensor(
                            guf[:, ic * TS:(ic + 1) * TS],
                            psB[:, ic * TS:(ic + 1) * TS],
                            wb[:, e * TC + tso: e * TC + tso + TS], ALU.mult))
                    SC.wait(vc, silu_val[u])
                    b.V(lambda p_=p_, n=ic_n * TS: vc.tensor_tensor(
                        gu[p_][:, 0:n], gs[p_][:, 0:n], guf[:, 0:n], ALU.mult))
                else:
                    SC.wait(vc, silu_val[u])
                    b.V(lambda p_=p_, n=ic_n * TS: vc.tensor_tensor(
                        gu[p_][:, 0:n], gs[p_][:, 0:n], psB[:, 0:n], ALU.mult))
                gu_ready[u] = VC.n

            def emit_C(u):
                e, ts = divmod(u, n_ts)
                sl_w = (e % 2)
                _, _, wdB = ew[sl_w]
                p_ = u % 2
                VC.wait(tn, gu_ready[u])
                if u >= 1:
                    VC.wait(tn, ydone[u - 1])
                for hcx in range(hc_n):
                    for ic in range(ic_n):
                        ins = tn.matmul(psC[:, hcx * TS:(hcx + 1) * TS],
                                        wdB[:, ic * D + hcx * 128: ic * D + (hcx + 1) * 128],
                                        gu[p_][:, ic * TS:(ic + 1) * TS],
                                        start=(ic == 0), stop=(ic == ic_n - 1))
                TN.inc(ins)
                cval[u] = TN.n
                b._slot_last_mm[sl_w] = TN.n

            def emit_D(u):
                e, ts = divmod(u, n_ts)
                tso = ts * TS
                TN.wait(vc, cval[u])
                if u < n_ts:
                    OUTS[0].wait(vc, x2_store.get(0, 0))
                    OUTS[1].wait(vc, x2_store.get(1, 0))
                yv = bass.AP(y_acc.ap().tensor, tso,
                             [[y_acc.ap().ap[0][0], 128], [TC, hc_n], [1, TS]])
                pv = bass.AP(psC.ap().tensor, 0,
                             [[psC.ap().ap[0][0], 128], [TS, hc_n], [1, TS]])
                if e == 0:
                    b.V(lambda: vc.tensor_copy(yv, pv))
                else:
                    b.V(lambda: vc.tensor_tensor(yv, yv, pv, ALU.add))
                ydone[u] = VC.n

            emit_A(0)
            for e in range(E):
                VC.wait(sy, wfrow_done)
                TN.wait(sy)
                sy.dma_start(wrowB1[0:1, :],
                             wFrow[e:e + 1, t0:t0 + TC]).then_inc(WR.h, 16)
                WR.n += 16
                WR.wait(tn)
                VC.wait(tn)
                for m in range((TC + 511) // 512):
                    w = min(512, TC - m * 512)
                    TN.inc(tn.matmul(psC[0:128, m * 512:m * 512 + w],
                                     ones1b[0:1, :],
                                     wrowB1[0:1, m * 512:m * 512 + w],
                                     start=True, stop=True))
                TN.wait(vc)
                b.V(lambda e=e: vc.tensor_copy(wb[:, e * TC:(e + 1) * TC],
                                               psC[:, 0:TC]))
            wb_ready = VC.n
            emit_B(0)

            for u in range(1, NU + 1):
                if u < NU:
                    emit_A(u)
                    emit_B(u)
                emit_C(u - 1)
                emit_D(u - 1)
                if u % n_ts == 0:
                    e_next = u // n_ts + 1
                    if e_next < nE:
                        prefetch_expert(e_next, e_next % 2)
                    elif tc + 1 < n_tc:
                        prefetch_expert(0, 0)
            ydone_v = ydone[NU - 1]
            b._prev_aval = aval_up[NU - 1]

            ntile3 = tsub_n
            r2_done = {}
            for j in range(ntile3):
                pst = psA if j % 2 == 0 else psB
                if j == 0:
                    VC.wait(tn, ydone_v)
                    SC.wait(tn)
                    VC.wait(tn)
                else:
                    VC.wait(tn, fin_read.get(j % 2, 0))
                for hcx in range(hc_n):
                    ins = tn.transpose(pst[:, hcx * 128:(hcx + 1) * 128],
                                       y_acc[:, hcx * TC + j * 128: hcx * TC + (j + 1) * 128],
                                       ident[:])
                TN.inc(ins)
                tdone = TN.n
                TN.wait(vc, tdone)
                X1R.wait(vc, d1)
                X1R2.wait(vc, d2)
                dst = x1c[slx][:, j * D:(j + 1) * D]
                b.V(lambda pst=pst, dst=dst: vc.tensor_add(dst, dst, pst[:, 0:D]))
                fin_read[j % 2] = VC.n
                r2_done[j] = VC.n
                nbs3 = (D + 511) // 512
                for c in range(nbs3):
                    w = min(512, D - c * 512)
                    b.V(lambda j=j, c=c, w=w, dst=dst, nbs3=nbs3: vc.bn_stats(
                        lnt38[:, 6 * (nbs3 * j + c):6 * (nbs3 * j + c) + 6],
                        dst[:, c * 512:c * 512 + w]))
            nbs3 = (D + 511) // 512
            mvo3 = 6 * nbs3 * ntile3
            for j in range(ntile3):
                b.V(lambda j=j, nbs3=nbs3, mvo3=mvo3: vc.bn_aggr(
                    lnt38[:, mvo3 + 2 * j:mvo3 + 2 * j + 2],
                    lnt38[:, 6 * nbs3 * j:6 * nbs3 * (j + 1)]))
            me3 = _ap2(lnt38, mvo3, [[2, ntile3]])
            va3 = _ap2(lnt38, mvo3 + 1, [[2, ntile3]])
            nm3 = lnt38[:, mvo3 + 2 * ntile3:mvo3 + 3 * ntile3]
            VC.wait(sc)
            b.S(lambda: sc.activation(va3, va3, AF.Ln, bias=consts[:, 0:1]))
            b.S(lambda: sc.activation(va3, va3, AF.Exp, scale=consts[:, 1:2]))
            SC.wait(vc)
            b.V(lambda: vc.tensor_tensor(nm3, me3, va3, ALU.mult))
            for j in range(ntile3):
                x2 = y_acc[:, j * D:(j + 1) * D]
                OUTS[j % 2].wait(vc, x2_store.get(j % 2, 0))
                b.V(lambda j=j, x2=x2, slx=slx: vc.tensor_scalar(
                    x2[:, 0:D], x1c[slx][:, j * D:(j + 1) * D],
                    lnt38[:, mvo3 + 1 + 2 * j:mvo3 + 2 + 2 * j], nm3[:, j:j + 1],
                    ALU.mult, ALU.subtract))
                VC.wait(sy)
                rows = slice(t0 + j * 128, t0 + (j + 1) * 128)
                sy.dma_start(out_h[rows, :], x2[:, 0:D]).then_inc(OUTS[j % 2].h, 16)
                OUTS[j % 2].n += 16
                x2_store[j % 2] = OUTS[j % 2].n
            b._x1c_read[slx] = VC.n

        OUTS[0].wait(sy)
        OUTS[1].wait(sy)

    nc.finalize()
    return nc


def _head_perm(H=16, KVH=4, Dh=64):
    rep = H // KVH
    qperm = []
    for p in range(H // 2):
        e = 2 * (p // rep) * rep + p % rep
        qperm += [e, e + rep]
    rows = []
    for h in qperm:
        rows += list(range(Dh * h, Dh * h + Dh))
    return qperm, np.array(rows)


def make_inputs(cfg, kw, batch_idx):
    qperm, qrows = _head_perm()
    rowperm = np.concatenate([qrows, np.arange(1024, 1536)])
    d = {}
    d["x"] = np.ascontiguousarray(kw["L0_attn_hidden_states"][batch_idx])
    d["wqkv"] = np.ascontiguousarray(np.asarray(kw["L0_attn_qkv_w"])[rowperm].T)
    d["bqkv"] = np.ascontiguousarray(np.asarray(kw["L0_attn_qkv_b"])[rowperm])
    d["scaling"] = np.ascontiguousarray(kw["L0_attn_scaling"])
    d["wo"] = np.ascontiguousarray(np.asarray(kw["L0_attn_o_w"])[:, qrows].T)
    d["bo"] = np.ascontiguousarray(kw["L0_attn_o_b"])
    d["wgate"] = np.ascontiguousarray(kw["L0_ffn_gate_w"])
    d["bgate"] = np.ascontiguousarray(kw["L0_ffn_gate_bias"])
    d["wg"] = np.ascontiguousarray(kw["L0_ffn_w_gate"])
    d["wu"] = np.ascontiguousarray(kw["L0_ffn_w_up"])
    d["wd"] = np.ascontiguousarray(kw["L0_ffn_w_down"])
    d["swg"] = np.ascontiguousarray(kw["L0_ffn_sw_gate"])
    d["swu"] = np.ascontiguousarray(kw["L0_ffn_sw_up"])
    d["swd"] = np.ascontiguousarray(kw["L0_ffn_sw_down"])
    return {k: np.ascontiguousarray(v).astype(np.float32) for k, v in d.items()}


def kernel(**inputs):
    cfg = full_cfg()
    nc = build_program(cfg)
    in_maps = [make_inputs(cfg, inputs, i) for i in range(8)]
    res = run_bass_kernel_spmd(nc, in_maps, core_ids=list(range(8)))
    out = np.stack([res.results[i]["out"] for i in range(8)], axis=0)
    return out.astype(np.float32)
